# version 21
# baseline (speedup 1.0000x reference)
import numpy as np
import ml_dtypes
import concourse.bass as bass
import concourse.mybir as mybir
from concourse.bass_utils import run_bass_kernel_spmd

F32 = mybir.dt.float32
BF16 = mybir.dt.bfloat16
AF = mybir.ActivationFunctionType
ALU = mybir.AluOpType
AXX = mybir.AxisListType.X

B, T, D = 2, 4096, 2048
TSL = 1024
NCORES = 8
SCALE = 1.0 / float(np.sqrt(128.0))
LAM0 = 0.2
RMS_EPS = 1.1920929e-07
LN_EPS = 1e-5
NE = 8
HH = 128
NT = 256
CAP = 384
NSB = CAP // 128

DEBUG_DUMPS = []



class _Op:
    __slots__ = ("eng", "fn", "deps", "inc", "attach_inc", "needed", "tgt", "sem_key", "tag", "phase")

    def __init__(self, eng, fn, deps, inc, tag, attach_inc=None):
        self.eng, self.fn, self.deps, self.inc, self.tag = eng, fn, deps, inc, tag
        self.attach_inc = attach_inc if attach_inc is not None else inc
        self.needed = False
        self.tgt = None
        self.sem_key = None
        self.phase = None


class Buf:
    def __init__(self, name=""):
        self.name = name
        self.w = None
        self.rs = []

    def rd_deps(self):
        return [self.w] if self.w is not None else []

    def wr_deps(self):
        d = [self.w] if self.w is not None else []
        return d + self.rs


class Em:
    ENGINES = ("sync", "scalar", "vector", "tensor", "gpsimd")

    def __init__(self, nc):
        self.nc = nc
        self.ops = []
        self.phase = "p0"

    def do(self, eng, fn, rd=(), wr=(), dma=False, cc=False, tag="", inc=None):
        deps, seen = [], set()
        for b in rd:
            for d in b.rd_deps():
                if id(d) not in seen:
                    seen.add(id(d))
                    deps.append(d)
        for b in wr:
            for d in b.wr_deps():
                if id(d) not in seen:
                    seen.add(id(d))
                    deps.append(d)
        op = _Op(eng, fn, deps, inc if inc is not None else (16 if dma else 1), tag,
                 attach_inc=(16 if dma else 1))
        op.phase = self.phase
        if cc:
            op.sem_key = ("CC", tag)
        elif dma:
            assert wr, f"dma op {tag} needs a target buf"
            op.sem_key = ("D", wr[0].name, "g" if eng == "gpsimd" else "h")
        else:
            op.sem_key = (eng, "cmp")
        self.ops.append(op)
        for b in rd:
            b.rs.append(op)
        for b in wr:
            b.w = op
            b.rs = []
        return op

    fence_fn = None
    fence_deps = ()

    def fence(self, eng, bufs, tag="fence"):
        deps, seen = [], set()
        for b in list(bufs) + list(self.fence_deps):
            for d in b.wr_deps():
                if id(d) not in seen:
                    seen.add(id(d))
                    deps.append(d)
        op = _Op(eng, self.fence_fn or (lambda e: e.nop(nofuse=True)), deps, 1, tag)
        op.sem_key = (eng, "cmp")
        self.ops.append(op)
        for b in self.fence_deps:
            b.w = op
            b.rs = []
        return op

    def overlay(self, new_bufs, old_bufs, tag="ovl"):
        op = self.fence("scalar", old_bufs, tag=tag)
        for b in new_bufs:
            b.w = op
            b.rs = []

    def final_fence(self, eng="scalar"):
        last = {}
        for op in self.ops:
            if op.sem_key[0] in ("D", "CC"):
                last[op.sem_key] = op
        deps = list(last.values())
        for b in self.fence_deps:
            deps.extend(b.wr_deps())
        op = _Op(eng, self.fence_fn or (lambda e: e.nop(nofuse=True)), deps, 1, "final")
        op.sem_key = (eng, "cmp")
        self.ops.append(op)
        for b in self.fence_deps:
            b.w = op
            b.rs = []
        return op

    def finalize(self, block, sem_of):
        for op in self.ops:
            op.deps = [d for d in op.deps
                       if not (d.eng == "tensor" and op.eng == "tensor")]
            for d in op.deps:
                d.needed = True
            if op.sem_key[0] in ("D", "CC"):
                op.needed = True
        counters = {}
        for op in self.ops:
            if op.needed:
                c = counters.get(op.sem_key, 0) + op.inc
                counters[op.sem_key] = c
                op.tgt = c
        per_eng = {e: [] for e in self.ENGINES}
        for op in self.ops:
            per_eng[op.eng].append(op)

        def body_for(eng):
            ops = per_eng[eng]

            def body(e):
                watermark = {}
                cur_phase = [None]
                from contextlib import ExitStack as _ES
                scope_stack = _ES()
                for op in ops:
                    if op.phase != cur_phase[0]:
                        scope_stack.close()
                        scope_stack = _ES()
                        if op.phase:
                            scope_stack.enter_context(self.nc.named_scope(op.phase))
                        cur_phase[0] = op.phase
                    waits = {}
                    for d in op.deps:
                        k = d.sem_key
                        if d.tgt > waits.get(k, 0):
                            waits[k] = d.tgt
                    for k in sorted(waits, key=str):
                        v = waits[k]
                        if watermark.get(k, 0) >= v:
                            continue
                        e.wait_ge(sem_of[k], v)
                        watermark[k] = v
                    try:
                        inst = op.fn(e)
                    except Exception as ex:
                        raise RuntimeError(f"emit failed: eng={op.eng} tag={op.tag} phase={op.phase}") from ex
                    if op.needed:
                        inst.then_inc(sem_of[op.sem_key], op.attach_inc)
                scope_stack.close()
            return body

        for eng in self.ENGINES:
            if per_eng[eng]:
                getattr(block, eng)(body_for(eng))

    def all_sem_keys(self):
        return {op.sem_key for op in self.ops}



def build_program(debug=()):
    nc = bass.Bass("TRN2", target_bir_lowering=False)
    em = Em(nc)

    d_xT = nc.dram_tensor("xT", [128, 16, T], F32, kind="ExternalInput")
    d_xresT = nc.dram_tensor("xresT", [128, 16, TSL], F32, kind="ExternalInput")
    d_xres = nc.dram_tensor("xres", [TSL, D], F32, kind="ExternalInput")
    d_wqkv = nc.dram_tensor("wqkvT", [128, 16, 1536], F32, kind="ExternalInput")
    d_woT = nc.dram_tensor("woT", [512, D], F32, kind="ExternalInput")
    d_rope = nc.dram_tensor("ropes", [128, 2, 4, 64], F32, kind="ExternalInput")
    d_lqk = nc.dram_tensor("lqk", [1, 1024], F32, kind="ExternalInput")
    d_gate = nc.dram_tensor("gate_s", [128, 16, NE], F32, kind="ExternalInput")
    d_alpha = nc.dram_tensor("alpha_s", [128, NE], F32, kind="ExternalInput")
    d_fc1 = nc.dram_tensor("fc1_s", [NE, 16, 128, 16, 128], F32, kind="ExternalInput")
    d_fc1b = nc.dram_tensor("fc1b_s", [128, NE * 16], F32, kind="ExternalInput")
    d_fc2 = nc.dram_tensor("fc2_s", [NE, 4, 2, 128, 8, 512], F32, kind="ExternalInput")
    d_fc2b = nc.dram_tensor("fc2b_s", [NE, D], F32, kind="ExternalInput")
    d_out = nc.dram_tensor("out", [TSL, D], F32, kind="ExternalOutput")

    d_pfq = [nc.dram_tensor(f"p_full{q}", [TSL, D], BF16, kind="Internal") for q in range(4)]
    d_rsq = [nc.dram_tensor(f"rs_out{q}", [256, D], BF16, kind="Internal") for q in range(4)]
    d_xtm = nc.dram_tensor("xn_tm", [TSL, D], F32, kind="Internal")
    d_toki = nc.dram_tensor("tok_i", [7296, 1], F32, kind="Internal")
    d_y2e = nc.dram_tensor("y2e", [NE * CAP, D], BF16, kind="Internal")

    tri = np.triu(np.ones((128, 128), dtype=np.float32)).astype(ml_dtypes.bfloat16)
    c_tri = nc.inline_tensor(tri, name="c_tri")
    c_id = nc.inline_tensor(np.eye(128, dtype=ml_dtypes.bfloat16), name="c_id")
    c_ones_r = nc.inline_tensor(np.ones((1, 128), dtype=ml_dtypes.bfloat16), name="c_ones_r")
    c_ones_rf = nc.inline_tensor(np.ones((1, 128), dtype=np.float32), name="c_ones_rf")
    c_idf = nc.inline_tensor(np.eye(128, dtype=np.float32), name="c_idf")
    c_mean_c = nc.inline_tensor(np.full((128, 1), 1.0 / D, dtype=ml_dtypes.bfloat16), name="c_mean_c")
    c_ones_c = nc.inline_tensor(np.ones((128, 1), dtype=ml_dtypes.bfloat16), name="c_ones_c")
    ind = np.zeros((8, NE * 128), dtype=np.float32)
    for e in range(NE):
        ind[e, e * 128:(e + 1) * 128] = 1.0
    c_ind = nc.inline_tensor(ind.astype(ml_dtypes.bfloat16), name="c_ind")
    c_fz = nc.inline_tensor(np.zeros((1, 4), dtype=np.float32), name="c_fz")
    c_eps = nc.inline_tensor(
        np.broadcast_to(np.array([LN_EPS, RMS_EPS, 0.0, 1.0], dtype=np.float32), (128, 4)).copy(),
        name="c_eps")
    c_iota8 = nc.inline_tensor(
        np.broadcast_to(np.arange(NE, dtype=np.float32), (128, NE)).copy(), name="c_iota8")
    ecol = np.tile(np.arange(NE, dtype=np.float32), 8)
    c_eCm1 = nc.inline_tensor(
        np.broadcast_to(ecol * CAP - 1.0, (128, 64)).copy(), name="c_eCm1")
    c_bnd = nc.inline_tensor(
        np.broadcast_to((ecol + 1.0) * CAP, (128, 64)).copy(), name="c_bnd")
    c_i128 = nc.inline_tensor(np.arange(128, dtype=np.float32)[:, None].copy(), name="c_i128")

    dbg_tensors = {}

    def dbg(name, shape, dtype=F32):
        if name in debug:
            dbg_tensors[name] = nc.dram_tensor("dbg_" + name, shape, dtype, kind="ExternalOutput")
            return dbg_tensors[name]
        return None

    ABS0 = (nc.sbuf_base + 31) & ~31
    _cursor = [ABS0 + 512]

    _perm_off = {}

    def perm(name, shape, dtype):
        nbytes = int(np.prod(shape[1:])) * mybir.dt.size(dtype)
        off = _cursor[0]
        _perm_off[name] = off
        _cursor[0] = (off + nbytes + 31) & ~31
        return nc.alloc_sbuf_tensor_at(name, shape, dtype, offset=off)

    def at(name, shape, dtype, off):
        return nc.alloc_sbuf_tensor_at(name, shape, dtype, offset=off)

    consts = perm("consts", [128, 392], BF16)
    constsf = perm("constsf", [128, 256], F32)
    ind_sb = perm("ind_sb", [8, NE * 128], BF16)
    lam_sb = perm("lam_sb", [1, 8], F32)
    nlam_col = perm("nlam_col", [128, 1], F32)
    ropes = perm("ropes_sb", [128, 512], BF16)
    lnrow = perm("lnrow", [1, 4 * 256], F32)
    qkraw = perm("qkraw", [128, 1024], F32)
    rms_sq = perm("rms_sq", [128, 512], F32)
    ssq = perm("ssq", [128, 8], F32)
    rstdqk = perm("rstdqk", [128, 8], F32)
    qn = perm("qn", [128, 512], BF16)
    kn = perm("kn", [128, 512], BF16)
    qr = perm("qr", [128, 512], F32)
    kr = perm("kr", [128, 512], F32)
    rot_t = perm("rot_t", [128, 256], F32)
    px = perm("px", [128, 2 * NT], F32)
    psq = perm("psq", [128, 2 * NT], F32)
    sqt = perm("sqt", [128, 2 * NT], F32)
    mcols = perm("mcols", [128, 8], F32)
    negw = perm("negw", [128, 1536], BF16)
    fc2b_sb = perm("fc2b_sb", [8, D], BF16)
    gate_sb = perm("gate_w_sb", [128, 16 * NE], F32)
    alpha_sb = perm("alpha_sb", [128, NE], F32)
    gtok = perm("gtok", [128, 8 * NE], F32)
    gprob = perm("gprob", [128, 8 * NE], F32)
    gred = perm("gred", [128, 8], F32)
    gmask = perm("gmask", [128, NE], F32)
    gtmp = perm("gtmp", [128, NE], F32)
    ctok = perm("ctok", [128, 8 * NE], F32)
    iota8c = perm("iota8c", [128, NE], F32)
    eCm1c = perm("eCm1c", [128, 64], F32)
    bndc = perm("bndc", [128, 64], F32)
    i128c = perm("i128c", [128, 1], F32)
    mtop = perm("mtop", [128, 64], F32)
    mtopb = perm("mtopb", [128, 64], BF16)
    gcum = perm("gcum", [128, 64], F32)

    m1b = perm("m1b", [128, 64], F32)
    m2b = perm("m2b", [128, 64], F32)
    grow = perm("grow", [1, 128], F32)
    mergec = perm("mergec", [128, 16], F32)
    gidx = perm("gidx", [128, 16], F32)
    gidxi = perm("gidxi", [128, 16], mybir.dt.int32)
    tkid = perm("tkid", [128, 8], F32)
    tki = perm("tki", [128, 24], F32)
    tkii = perm("tkii", [128, 24], mybir.dt.int32)

    fc1b_sb = perm("fc1b_sb", [128, NE * 16], F32)
    gelu_sc = perm("gelu_sc", [128, 512], BF16)
    rrow = perm("rrow", [1, 512], F32)
    rrowb = perm("rrowb", [1, 512], BF16)
    offr = at("offr", [128, 64], F32, _perm_off["gprob"])
    mrowb = at("mrowb", [1, 256], BF16, _perm_off["gelu_sc"])
    sgix = at("sgix", [128, 16], F32, _perm_off["mtopb"])
    sgixi = at("sgixi", [128, 16], mybir.dt.int32, _perm_off["mtopb"] + 64)
    wsumb = at("wsumb", [1, 1536], BF16, _perm_off["rrow"])
    lam_prod = at("lam_prod", [1, 256], F32, _perm_off["qkraw"])
    fdum = perm("fdum", [1, 4], F32)
    epsc = perm("epsc", [128, 4], F32)
    nc.const_aps.register(F32, 0.0, epsc[:, 2:3])
    nc.const_aps.register(F32, 1.0, epsc[:, 3:4])
    PERM_END = _cursor[0]
    assert PERM_END <= ABS0 + 45056, (PERM_END, ABS0)

    OF = ABS0 + 45056
    qF = at("qF", [128, 4 * T], BF16, OF)
    kF = at("kF", [128, 4 * T], BF16, OF + 32768)
    vTok = at("vTok", [128, 32 * 512], BF16, OF + 65536)
    wqkv = at("wqkv", [128, 16 * 1536], BF16, OF + 98304)
    xch = [at("xch0", [128, 16 * NT], BF16, OF + 147456),
           at("xch1", [128, 16 * NT], BF16, OF + 155648)]
    assert OF + 163840 <= nc.sbuf_top, (OF, nc.sbuf_top)
    yF = at("yF", [128, 4 * T], BF16, OF + 98304)
    etile = [at(f"etile{i}", [128, 512], BF16, OF + 131072 + i * 1024) for i in range(4)]
    rrep = at("rrep", [128, 512], F32, OF + 135168)
    t2sb = at("t2sb", [128, 2 * 512], F32, OF + 137216)
    usb = at("usb", [128, 512], F32, OF + 141312)
    esum = [at(f"esum{i}", [128, 512], F32, OF + 143360 + i * 2048) for i in range(2)]
    wosl = [at(f"wosl{i}", [128, D], BF16, OF + 131072 + i * 4096) for i in range(4)]
    pout8 = at("pout8", [128, 16 * 512], BF16, OF + 65536)
    xg_tm = at("xg_tm", [128, NSB * D], F32, OF)
    xge = [at(f"xge{i}", [128, 16 * CAP], BF16, OF + 24576 + i * 12288) for i in range(2)]
    y2sb = [at(f"y2sb{i}", [128, NSB * D], BF16, OF + 49152 + i * 12288) for i in range(2)]
    mgt = [at(f"mgt{i}", [128, D], BF16, OF + 73728 + i * 4096) for i in range(4)]
    hsb = [at(f"hsb{i}", [128, 16 * CAP], BF16, OF + 98304 + i * 12288) for i in range(2)]
    fc1sl = [at(f"fc1sl{i}", [128, 16 * 128], BF16, OF + 147456 + i * 4096) for i in range(4)]
    fc2sl = [at(f"fc2sl{i}", [128, 8 * 512], BF16, OF + 131072 + i * 8192) for i in range(2)]
    xtf = at("xtf", [128, 2 * NT], F32, _perm_off["qn"])
    xtm = [at(f"xtm{i}", [128, 512], F32, _perm_off["qr"] + i * 2048) for i in range(2)]
    xrtile = [at(f"xrtile{i}", [128, D], F32, OF + 98304 + i * 8192) for i in range(2)]
    rtile = [at(f"rtile{i}", [128, D], BF16, OF + 114688 + i * 4096) for i in range(2)]

    from contextlib import ExitStack
    stack = ExitStack()
    en = stack.enter_context
    with stack:
        PS = [en(nc.psum_tensor(f"PS{i}", [128, 512], F32)) for i in range(8)]
        block = en(nc.Block())

        bufs = {}

        def Bf(name):
            if name not in bufs:
                bufs[name] = Buf(name)
            return bufs[name]

        pbuf = [Bf(f"PS{i}") for i in range(8)]
        fd_b = Bf("fdum")
        em.do("sync", lambda e: e.dma_start(out=fdum[:], in_=c_fz.ap()), wr=[fd_b], dma=True, tag="fdum_init")
        em.fence_fn = lambda e: e.copy(fdum[0:1, 0:1], fdum[0:1, 1:2])
        em.fence_deps = (fd_b,)

        c_b = Bf("consts")
        em.do("sync", lambda e: e.dma_start(out=consts[:, 0:128], in_=c_tri.ap()), wr=[c_b], dma=True)
        em.do("sync", lambda e: e.dma_start(out=consts[:, 128:256], in_=c_id.ap()), wr=[c_b], dma=True)
        em.do("sync", lambda e: e.dma_start(out=consts[:, 256:257], in_=c_mean_c.ap()), wr=[c_b], dma=True)
        em.do("sync", lambda e: e.dma_start(out=consts[:, 257:258], in_=c_ones_c.ap()), wr=[c_b], dma=True)
        em.do("sync", lambda e: e.dma_start(out=consts[0:1, 258:386], in_=c_ones_r.ap()), wr=[c_b], dma=True)
        TRI = consts[:, 0:128]
        IDN = consts[:, 128:256]
        MEANC = consts[:, 256:257]
        ONESC = consts[:, 257:258]
        ONESR = consts[0:1, 258:386]
        cf_b = Bf("constsf")
        em.do("sync", lambda e: e.dma_start(out=constsf[0:1, 0:128], in_=c_ones_rf.ap()), wr=[cf_b], dma=True)
        em.do("sync", lambda e: e.dma_start(out=constsf[:, 128:256], in_=c_idf.ap()), wr=[cf_b], dma=True)
        ONESRF = constsf[0:1, 0:128]
        IDNF = constsf[:, 128:256]
        ind_b = Bf("ind_sb")
        em.do("sync", lambda e: e.dma_start(out=ind_sb[:], in_=c_ind.ap()), wr=[ind_b], dma=True)
        eps_b = Bf("epsc")
        em.do("sync", lambda e: e.dma_start(out=epsc[:], in_=c_eps.ap()), wr=[eps_b], dma=True)
        rt_c = Bf("routing_consts")
        em.do("sync", lambda e: e.dma_start(out=iota8c[:], in_=c_iota8.ap()), wr=[rt_c], dma=True)
        em.do("sync", lambda e: e.dma_start(out=eCm1c[:], in_=c_eCm1.ap()), wr=[rt_c], dma=True)
        em.do("sync", lambda e: e.dma_start(out=bndc[:], in_=c_bnd.ap()), wr=[rt_c], dma=True)
        em.do("sync", lambda e: e.dma_start(out=i128c[:], in_=c_i128.ap()), wr=[rt_c], dma=True)
        mt_b = Bf("mtop")
        em.do("vector", lambda e: e.memset(mtop[:], 0.0), wr=[mt_b])
        toki_b = Bf("tok_i")
        em.do("sync", lambda e: e.dma_start(
            out=d_toki.ap()[0:NE * CAP, :].rearrange("(p c) one -> p (c one)", p=128),
            in_=mtop[:, 0:NE * NSB]),
            rd=[mt_b], wr=[toki_b], dma=True, tag="toki_zero")

        rope_b = Bf("ropes")
        em.do("gpsimd", lambda e: e.dma_start(
            out=ropes[:], in_=d_rope.ap().rearrange("p a b c -> p (a b c)")), wr=[rope_b], dma=True)
        lqk_b = Bf("lnrow")
        em.do("sync", lambda e: e.dma_start(out=lnrow[:], in_=d_lqk.ap()), wr=[lqk_b], dma=True)
        gate_b, alpha_b = Bf("gate"), Bf("alpha")
        em.do("gpsimd", lambda e: e.dma_start(
            out=gate_sb[:], in_=d_gate.ap().rearrange("p a b -> p (a b)")), wr=[gate_b], dma=True)
        em.do("sync", lambda e: e.dma_start(out=alpha_sb[:], in_=d_alpha.ap()), wr=[alpha_b], dma=True)
        fc1b_b, fc2b_b = Bf("fc1b"), Bf("fc2b")
        em.do("sync", lambda e: e.dma_start(out=fc1b_sb[:], in_=d_fc1b.ap()), wr=[fc1b_b], dma=True)
        em.do("gpsimd", lambda e: e.dma_start(out=fc2b_sb[:], in_=d_fc2b.ap()), wr=[fc2b_b], dma=True)
        wqkv_b = Bf("wqkv")
        for q4 in range(4):
            em.do("gpsimd", lambda e, q4=q4: e.dma_start(
                out=wqkv[:].rearrange("p (jc w) -> p jc w", jc=16)[:, q4 * 4:(q4 + 1) * 4, :],
                in_=d_wqkv.ap()[:, q4 * 4:(q4 + 1) * 4, :]), wr=[wqkv_b], dma=True, tag="wqkv_ld")

        lam_b, lpr_b, nlam_b = Bf("lam"), Bf("lam_prod"), Bf("nlam")
        em.do("vector", lambda e: e.tensor_mul(lam_prod[0:1, 0:256], lnrow[0:1, 0:256], lnrow[0:1, 256:512]),
              rd=[lqk_b], wr=[lpr_b])
        em.do("vector", lambda e: e.reduce_sum(lam_sb[0:1, 0:1], lam_prod[0:1, 0:256], axis=AXX),
              rd=[lpr_b], wr=[lam_b])
        em.do("vector", lambda e: e.tensor_mul(lam_prod[0:1, 0:256], lnrow[0:1, 512:768], lnrow[0:1, 768:1024]),
              rd=[lqk_b, lam_b], wr=[lpr_b])
        em.do("vector", lambda e: e.reduce_sum(lam_sb[0:1, 1:2], lam_prod[0:1, 0:256], axis=AXX),
              rd=[lpr_b], wr=[lam_b])
        em.do("scalar", lambda e: e.activation(lam_sb[0:1, 2:4], lam_sb[0:1, 0:2], AF.Exp),
              rd=[lam_b, eps_b], wr=[lam_b])
        em.do("vector", lambda e: e.tensor_sub(lam_sb[0:1, 4:5], lam_sb[0:1, 3:4], lam_sb[0:1, 2:3]),
              rd=[lam_b], wr=[lam_b])
        em.do("vector", lambda e: e.tensor_scalar_add(lam_sb[0:1, 5:6], lam_sb[0:1, 4:5], -LAM0),
              rd=[lam_b], wr=[lam_b])
        em.do("tensor", lambda e: e.matmul(PS[7][:, 0:1], ONESRF, lam_sb[0:1, 5:6], start=True, stop=True),
              rd=[lam_b, cf_b], wr=[pbuf[7]])
        em.do("scalar", lambda e: e.copy(nlam_col[:], PS[7][:, 0:1]), rd=[pbuf[7], eps_b], wr=[nlam_b])

        ONESF = epsc[:, 3:4]
        wsb_b, negw_b = Bf("wsumb"), Bf("negw")
        for co3 in range(3):
            for jc in range(16):
                rhs = wqkv[:].rearrange("p (jc w) -> p jc w", jc=16)[:, jc, co3 * 512:(co3 + 1) * 512]
                em.do("tensor", lambda e, rhs=rhs, co3=co3, jc=jc: e.matmul(
                    PS[co3][0:1, 0:512], ONESC, rhs, start=(jc == 0), stop=(jc == 15)),
                    rd=[wqkv_b, c_b], wr=[pbuf[co3]], tag="wsum_mm")
            em.do("scalar", lambda e, co3=co3: e.activation(
                wsumb[0:1, co3 * 512:(co3 + 1) * 512], PS[co3][0:1, 0:512], AF.Copy, scale=-1.0),
                rd=[pbuf[co3], eps_b], wr=[wsb_b], tag="wsum_ev")
            em.do("tensor", lambda e, co3=co3: e.matmul(
                PS[3 + co3][:], ONESR, wsumb[0:1, co3 * 512:(co3 + 1) * 512], start=True, stop=True),
                rd=[wsb_b, c_b], wr=[pbuf[3 + co3]], tag="negw_mm")
            em.do("vector", lambda e, co3=co3: e.tensor_copy(
                negw[:, co3 * 512:(co3 + 1) * 512], PS[3 + co3][:]),
                rd=[pbuf[3 + co3]], wr=[negw_b], tag="negw_ev")

        lnrow_b = lqk_b
        px_b, psq_b, mcol_b = Bf("px"), Bf("psq"), Bf("mcols")
        sqt_b = [Bf("sqt0"), Bf("sqt1")]

        def stats_partials(xv, xin_b, cp, n, xt=None):
            pxv = px[:, cp * NT: cp * NT + n]
            psqv = psq[:, cp * NT: cp * NT + n]
            xf = (xt if xt is not None else xch[cp])[:].rearrange(
                "p (jc w) -> p w jc", jc=16)
            em.do("vector", lambda e, xf=xf, pxv=pxv: e.tensor_reduce(
                pxv, xf, axis=AXX, op=ALU.add), rd=[xin_b], wr=[px_b], tag="st_x")
            for jc in range(16):
                x = xv(jc)
                if jc == 0:
                    em.do("scalar", lambda e, x=x, psqv=psqv: e.activation(psqv, x, AF.Square),
                          rd=[xin_b, eps_b], wr=[psq_b], tag="st_sq0")
                else:
                    pj = jc % 2
                    sv = sqt[:, pj * NT:pj * NT + n]
                    em.do("scalar", lambda e, x=x, sv=sv: e.activation(sv, x, AF.Square),
                          rd=[xin_b, eps_b], wr=[sqt_b[pj]], tag="st_sq")
                    em.do("gpsimd", lambda e, psqv=psqv, sv=sv: e.tensor_add(psqv, psqv, sv),
                          rd=[sqt_b[pj], psq_b], wr=[psq_b], tag="st_sqa")

        def stats_finish(cp, n):
            pxv = px[:, cp * NT: cp * NT + n]
            psqv = psq[:, cp * NT: cp * NT + n]
            em.do("tensor", lambda e, pxv=pxv, n=n: e.matmul(
                PS[0][0:1, 0:n], ONESF, pxv, start=True, stop=True),
                rd=[px_b, eps_b], wr=[pbuf[0]], tag="st_mm_m")
            em.do("tensor", lambda e, psqv=psqv, n=n: e.matmul(
                PS[1][0:1, 0:n], ONESF, psqv, start=True, stop=True),
                rd=[psq_b, eps_b], wr=[pbuf[1]], tag="st_mm_sq")
            em.do("scalar", lambda e, n=n: e.activation(lnrow[0:1, 0:n], PS[0][0:1, 0:n],
                                                        AF.Copy, scale=1.0 / D),
                  rd=[pbuf[0], eps_b], wr=[lnrow_b], tag="row_m")
            em.do("scalar", lambda e, n=n: e.activation(lnrow[0:1, 256:256 + n], PS[1][0:1, 0:n],
                                                        AF.Copy, scale=1.0 / D),
                  rd=[pbuf[1], eps_b], wr=[lnrow_b], tag="row_msq")
            em.do("vector", lambda e, n=n: e.tensor_mul(lnrow[0:1, 768:768 + n], lnrow[0:1, 0:n],
                                                        lnrow[0:1, 0:n]), rd=[lnrow_b], wr=[lnrow_b])
            em.do("vector", lambda e, n=n: e.tensor_sub(lnrow[0:1, 256:256 + n], lnrow[0:1, 256:256 + n],
                                                        lnrow[0:1, 768:768 + n]), rd=[lnrow_b], wr=[lnrow_b])
            em.do("scalar", lambda e, n=n: e.activation(lnrow[0:1, 512:512 + n], lnrow[0:1, 256:256 + n],
                                                        AF.Ln, bias=epsc[0:1, 0:1]), rd=[lnrow_b, eps_b], wr=[lnrow_b])
            em.do("scalar", lambda e, n=n: e.activation(lnrow[0:1, 512:512 + n], lnrow[0:1, 512:512 + n],
                                                        AF.Exp, scale=-0.5), rd=[lnrow_b, eps_b], wr=[lnrow_b])
            em.do("scalar", lambda e, n=n: e.activation(mrowb[0:1, 0:n], lnrow[0:1, 0:n], AF.Copy),
                  rd=[lnrow_b, eps_b], wr=[Bf("mrowb")], tag="mrow_cast")

        def cols_tb(n_off, p):
            for ri, src in ((1, 512),):
                em.do("tensor", lambda e, ri=ri, src=src, n_off=n_off: e.matmul(
                    PS[2 + ri][:, 0:1], lnrow[0:1, src + n_off: src + n_off + 128],
                    epsc[0:1, 3:4], start=True, stop=True),
                    rd=[lnrow_b, eps_b], wr=[pbuf[2 + ri]], tag="col_tr")
                em.do("scalar", lambda e, ri=ri, p=p: e.copy(
                    mcols[:, p * 4 + ri: p * 4 + ri + 1], PS[2 + ri][:, 0:1]),
                    rd=[pbuf[2 + ri], eps_b], wr=[mcol_b], tag="col_ev")

        def xv16(t, w):
            return t[:].rearrange("p (jc w) -> p jc w", jc=16, w=w)

        em.phase = "p12"
        xch_b = [Bf("xch0"), Bf("xch1")]
        qF_b, kF_b, vT_b, yF_b = Bf("qF"), Bf("kF"), Bf("vTok"), Bf("yF")
        qkraw_b, rsq_b, ssq_b, rstd_b = Bf("qkraw"), Bf("rms_sq"), Bf("ssq"), Bf("rstdqk")
        qn_b, kn_b, qr_b, kr_b, rot_b = Bf("qn"), Bf("kn"), Bf("qr"), Bf("kr"), Bf("rot_t")

        COS = ropes[:].rearrange("p (s b c) -> p s b c", s=2, b=4)[:, 0]
        SIN = ropes[:].rearrange("p (s b c) -> p s b c", s=2, b=4)[:, 1]

        pend_tr = [None]

        def emit_tr(tb):
            for (src, sb_, dstF, dF_b) in ((qr, qr_b, qF, qF_b), (kr, kr_b, kF, kF_b)):
                for chb in range(4):
                    pt = PS[7][:, chb * 128:(chb + 1) * 128]
                    em.do("tensor", lambda e, src=src, chb=chb, pt=pt: e.transpose(
                        pt, src[:, chb * 128:(chb + 1) * 128], IDNF),
                        rd=[sb_, cf_b], wr=[pbuf[7]], tag="qk_tr")
                ov = dstF[:].rearrange("p (c t) -> p c t", c=4)[:, :, tb * 128:(tb + 1) * 128]
                em.do("vector", lambda e, ov=ov: e.tensor_copy(ov, PS[7][:].rearrange(
                    "p (c t) -> p c t", c=4)), rd=[pbuf[7]], wr=[dF_b], tag="qk_tr_ev")

        for tc in range(T // NT):
            cp = tc % 2
            xb = xch_b[cp]
            for q4 in range(4):
                em.do("gpsimd", lambda e, tc=tc, q4=q4, cp=cp: e.dma_start(
                    out=xv16(xch[cp], NT)[:, q4 * 4:(q4 + 1) * 4, :],
                    in_=d_xT.ap()[:, q4 * 4:(q4 + 1) * 4, tc * NT:(tc + 1) * NT]),
                    wr=[xb], dma=True, tag="x_load")
            stats_partials(lambda jc, cp=cp: xch[cp][:, jc * NT:(jc + 1) * NT], xb, cp, NT)

            for tbl in range(NT // 128):
                tb = tc * (NT // 128) + tbl
                p = tbl
                for jc in range(16):
                    lhs = xv16(xch[cp], NT)[:, jc, tbl * 128:(tbl + 1) * 128]
                    for (ps_i, co) in ((4, 0), (5, 512), (6, 1024)):
                        rhs = xv16(wqkv, 1536)[:, jc, co:co + 512]
                        em.do("tensor", lambda e, lhs=lhs, rhs=rhs, ps_i=ps_i, jc=jc: e.matmul(
                            PS[ps_i][:], lhs, rhs, start=(jc == 0), stop=False),
                            rd=[xb, wqkv_b], wr=[pbuf[ps_i]], tag="proj_mm")
                if tbl == 0:
                    stats_finish(cp, NT)
                if pend_tr[0] is not None:
                    emit_tr(pend_tr[0])
                    pend_tr[0] = None
                cols_tb(tbl * 128, p)
                for (ps_i, co) in ((4, 0), (5, 512), (6, 1024)):
                    em.do("tensor", lambda e, ps_i=ps_i, co=co, tbl=tbl: e.matmul(
                        PS[ps_i][:], mrowb[0:1, tbl * 128:(tbl + 1) * 128],
                        wsumb[0:1, co:co + 512], start=False, stop=True),
                        rd=[Bf("mrowb"), wsb_b], wr=[pbuf[ps_i]], tag="mcorr_mm")
                rc = mcols[:, p * 4 + 1: p * 4 + 2]
                em.do("scalar", lambda e: e.copy(qkraw[:, 0:512], PS[4][:]),
                      rd=[pbuf[4], eps_b], wr=[qkraw_b], tag="qk_ev")
                em.do("scalar", lambda e: e.copy(qkraw[:, 512:1024], PS[5][:]),
                      rd=[pbuf[5], eps_b], wr=[qkraw_b], tag="qk_ev")
                vv = vTok[:, tb * 512:(tb + 1) * 512]
                em.do("vector", lambda e, vv=vv, rc=rc: e.tensor_scalar_mul(vv, PS[6][:], rc),
                      rd=[pbuf[6], mcol_b], wr=[vT_b], tag="v_scale")
                em.do("vector", lambda e: e.tensor_mul(rms_sq[:], qkraw[:, 0:512], qkraw[:, 0:512]),
                      rd=[qkraw_b], wr=[rsq_b], tag="rms_sq_q")
                em.do("vector", lambda e: e.tensor_reduce(
                    ssq[:, 0:4], rms_sq[:].rearrange("p (b c) -> p b c", b=4), axis=AXX, op=ALU.add),
                    rd=[rsq_b], wr=[ssq_b], tag="rms_red_q")
                em.do("vector", lambda e: e.tensor_mul(rms_sq[:], qkraw[:, 512:1024], qkraw[:, 512:1024]),
                      rd=[qkraw_b], wr=[rsq_b], tag="rms_sq_k")
                em.do("vector", lambda e: e.tensor_reduce(
                    ssq[:, 4:8], rms_sq[:].rearrange("p (b c) -> p b c", b=4), axis=AXX, op=ALU.add),
                    rd=[rsq_b], wr=[ssq_b], tag="rms_red_k")
                em.do("scalar", lambda e: e.activation(rstdqk[:], ssq[:], AF.Ln, scale=1.0 / HH, bias=epsc[:, 1:2]),
                      rd=[ssq_b, eps_b], wr=[rstd_b])
                em.do("scalar", lambda e: e.activation(rstdqk[:], rstdqk[:], AF.Exp, scale=-0.5),
                      rd=[rstd_b, eps_b], wr=[rstd_b])
                for blk in range(4):
                    em.do("vector", lambda e, blk=blk: e.tensor_scalar_mul(
                        qn[:, blk * 128:(blk + 1) * 128], qkraw[:, blk * 128:(blk + 1) * 128],
                        rstdqk[:, blk:blk + 1]), rd=[qkraw_b, rstd_b], wr=[qn_b], tag="rms_ap_q")
                    em.do("vector", lambda e, blk=blk: e.tensor_scalar_mul(
                        kn[:, blk * 128:(blk + 1) * 128], qkraw[:, 512 + blk * 128:512 + (blk + 1) * 128],
                        rstdqk[:, 4 + blk:5 + blk]), rd=[qkraw_b, rstd_b], wr=[kn_b], tag="rms_ap_k")
                for (src, sb_, dst, db_) in ((qn, qn_b, qr, qr_b), (kn, kn_b, kr, kr_b)):
                    s4 = src[:].rearrange("p (b h c) -> p b h c", b=4, h=2)
                    d4 = dst[:].rearrange("p (b h c) -> p b h c", b=4, h=2)
                    x1, x2 = s4[:, :, 0], s4[:, :, 1]
                    o1, o2 = d4[:, :, 0], d4[:, :, 1]
                    rt = rot_t[:].rearrange("p (b c) -> p b c", b=4)
                    em.do("vector", lambda e, x1=x1, o1=o1: e.tensor_mul(o1, x1, COS),
                          rd=[sb_, rope_b], wr=[db_], tag="rot")
                    em.do("vector", lambda e, x2=x2, rt=rt: e.tensor_mul(rt, x2, SIN),
                          rd=[sb_, rope_b], wr=[rot_b], tag="rot")
                    em.do("vector", lambda e, o1=o1, rt=rt: e.tensor_add(o1, o1, rt),
                          rd=[rot_b, db_], wr=[db_], tag="rot")
                    em.do("vector", lambda e, x2=x2, o2=o2: e.tensor_mul(o2, x2, COS),
                          rd=[sb_, rope_b], wr=[db_], tag="rot")
                    em.do("vector", lambda e, x1=x1, rt=rt: e.tensor_mul(rt, x1, SIN),
                          rd=[sb_, rope_b], wr=[rot_b], tag="rot")
                    em.do("vector", lambda e, o2=o2, rt=rt: e.tensor_sub(o2, o2, rt),
                          rd=[rot_b, db_], wr=[db_], tag="rot")
                pend_tr[0] = tb
        if pend_tr[0] is not None:
            emit_tr(pend_tr[0])
            pend_tr[0] = None

        em.phase = "p12b"
        gt_b = Bf("gtok")
        xtmd_b = Bf("xn_tm")
        mr_b = Bf("mrrep")
        xtf_b = [Bf("xtf0"), Bf("xtf1")]
        xtm_b = [Bf("xtm0"), Bf("xtm1")]
        em.overlay([mr_b], [rsq_b], tag="ovl_mr")
        em.overlay(xtf_b, [qn_b, kn_b], tag="ovl_xtf")
        em.overlay(xtm_b, [qr_b, kr_b], tag="ovl_xtm")
        mrrep = rms_sq

        def emit_slice_chunk(k):
            em.phase = "p12b"
            cp = k % 2
            xb = xch_b[cp]
            for q4 in range(4):
                em.do("gpsimd", lambda e, k=k, q4=q4, cp=cp: e.dma_start(
                    out=xv16(xch[cp], NT)[:, q4 * 4:(q4 + 1) * 4, :],
                    in_=d_xresT.ap()[:, q4 * 4:(q4 + 1) * 4, k * NT:(k + 1) * NT]),
                    wr=[xb], dma=True, tag="xsl_load")
            stats_partials(lambda jc, cp=cp: xch[cp][:, jc * NT:(jc + 1) * NT], xb, cp, NT)
            stats_finish(cp, NT)
            for ri, src in ((0, 0), (1, 512)):
                em.do("tensor", lambda e, ri=ri, src=src: e.matmul(
                    PS[2 + ri][:, 0:NT], ONESRF, lnrow[0:1, src:src + NT], start=True, stop=True),
                    rd=[lnrow_b, cf_b], wr=[pbuf[2 + ri]], tag="mr_rep")
                em.do("scalar", lambda e, ri=ri: e.copy(mrrep[:, ri * NT:(ri + 1) * NT], PS[2 + ri][:, 0:NT]),
                      rd=[pbuf[2 + ri], eps_b], wr=[mr_b], tag="mr_ev")
            for jc in range(16):
                pj = jc % 2
                xt = xtf[:, pj * NT:(pj + 1) * NT]
                x = xch[cp][:, jc * NT:(jc + 1) * NT]
                em.do("vector", lambda e, xt=xt, x=x: e.tensor_sub(xt, x, mrrep[:, 0:NT]),
                      rd=[xb, mr_b], wr=[xtf_b[pj]], tag="xn_sub")
                em.do("gpsimd", lambda e, xt=xt: e.tensor_mul(xt, xt, mrrep[:, NT:2 * NT]),
                      rd=[mr_b, xtf_b[pj]], wr=[xtf_b[pj]], tag="xn_mul")
                for tbl in range(2):
                    lhs = xtf[:, pj * NT + tbl * 128: pj * NT + (tbl + 1) * 128]
                    em.do("tensor", lambda e, lhs=lhs, jc=jc, tbl=tbl: e.matmul(
                        PS[4 + tbl][:, 0:NE], lhs, gate_sb[:, jc * NE:(jc + 1) * NE],
                        start=(jc == 0), stop=(jc == 15)),
                        rd=[xtf_b[pj], gate_b], wr=[pbuf[4 + tbl]], tag="gate_mm")
                    pt = PS[7 - tbl][:, (jc % 4) * 128:(jc % 4 + 1) * 128]
                    em.do("tensor", lambda e, lhs=lhs, pt=pt: e.transpose(pt, lhs, IDNF),
                          rd=[xtf_b[pj], cf_b], wr=[pbuf[7 - tbl]], tag="xn_tr")
                if jc % 4 == 3:
                    for tbl in range(2):
                        em.do("vector", lambda e, tbl=tbl: e.tensor_copy(xtm[tbl][:], PS[7 - tbl][:]),
                              rd=[pbuf[7 - tbl]], wr=[xtm_b[tbl]], tag="xn_tr_ev")
                        tok0 = (k * 2 + tbl) * 128
                        c0 = (jc - 3) * 128
                        em.do("sync", lambda e, tbl=tbl, tok0=tok0, c0=c0: e.dma_start(
                            out=d_xtm.ap()[tok0:tok0 + 128, c0:c0 + 512], in_=xtm[tbl][:]),
                            rd=[xtm_b[tbl]], wr=[xtmd_b], dma=True, tag="xtm_store")
            for tbl in range(2):
                tb = k * 2 + tbl
                em.do("scalar", lambda e, tb=tb, tbl=tbl: e.copy(
                    gtok[:, tb * NE:(tb + 1) * NE], PS[4 + tbl][:, 0:NE]),
                    rd=[pbuf[4 + tbl], eps_b], wr=[gt_b], tag="gate_ev")
            emit_gate_tb(k * 2)
            emit_gate_tb(k * 2 + 1)
            em.phase = "p3"

        gp_b, gr_b, gm_b, gtmp_b = Bf("gprob"), Bf("gred"), Bf("gmask"), Bf("gtmp")
        m1_b, m2_b = Bf("m1b"), Bf("m2b")

        def emit_gate_tb(tb):
            em.phase = "p12c"
            g = gtok[:, tb * NE:(tb + 1) * NE]
            p = gprob[:, tb * NE:(tb + 1) * NE]
            c = ctok[:, tb * NE:(tb + 1) * NE]
            m1v = m1b[:, tb * NE:(tb + 1) * NE]
            m2v = m2b[:, tb * NE:(tb + 1) * NE]
            mtv = mtop[:, tb * NE:(tb + 1) * NE]
            em.do("vector", lambda e, g=g: e.reduce_max(gred[:, 0:1], g, axis=AXX), rd=[gt_b], wr=[gr_b])
            em.do("vector", lambda e, g=g, p=p: e.tensor_scalar(p, g, gred[:, 0:1], None, ALU.subtract),
                  rd=[gt_b, gr_b], wr=[gp_b])
            em.do("scalar", lambda e, p=p: e.activation(p, p, AF.Exp), rd=[gp_b, eps_b], wr=[gp_b])
            em.do("vector", lambda e, p=p: e.reduce_sum(gred[:, 1:2], p, axis=AXX), rd=[gp_b], wr=[gr_b])
            em.do("vector", lambda e: e.reciprocal(gred[:, 2:3], gred[:, 1:2]), rd=[gr_b], wr=[gr_b])
            em.do("vector", lambda e, p=p: e.tensor_scalar(p, p, gred[:, 2:3], None, ALU.mult),
                  rd=[gp_b, gr_b], wr=[gp_b])
            em.do("vector", lambda e, p=p: e.reduce_max(gred[:, 3:4], p, axis=AXX), rd=[gp_b], wr=[gr_b])
            em.do("vector", lambda e, p=p, m1v=m1v: e.tensor_scalar(m1v, p, gred[:, 3:4], None, ALU.is_ge),
                  rd=[gp_b, gr_b], wr=[m1_b], tag="m1")
            em.do("vector", lambda e, p=p: e.tensor_scalar(gmask[:], p, gred[:, 3:4], None, ALU.is_lt),
                  rd=[gp_b, gr_b], wr=[gm_b])
            em.do("vector", lambda e, p=p: e.tensor_mul(gtmp[:], p, gmask[:]), rd=[gp_b, gm_b], wr=[gtmp_b])
            em.do("vector", lambda e: e.reduce_max(gred[:, 4:5], gtmp[:], axis=AXX), rd=[gtmp_b], wr=[gr_b])
            em.do("vector", lambda e, p=p, mtv=mtv: e.tensor_scalar(mtv, p, gred[:, 4:5], None, ALU.is_ge),
                  rd=[gp_b, gr_b], wr=[mt_b], tag="mtop")
            em.do("vector", lambda e, p=p, mtv=mtv, m1v=m1v, m2v=m2v: e.tensor_sub(m2v, mtv, m1v),
                  rd=[mt_b, m1_b], wr=[m2_b], tag="m2")
            em.do("vector", lambda e, p=p, mtv=mtv: e.tensor_mul(gtmp[:], p, mtv),
                  rd=[gp_b, mt_b], wr=[gtmp_b])
            em.do("vector", lambda e: e.reduce_sum(gred[:, 5:6], gtmp[:], axis=AXX), rd=[gtmp_b], wr=[gr_b])
            em.do("vector", lambda e: e.reciprocal(gred[:, 6:7], gred[:, 5:6]), rd=[gr_b], wr=[gr_b])
            em.do("vector", lambda e: e.tensor_scalar(gtmp[:], gtmp[:], gred[:, 6:7], None, ALU.mult),
                  rd=[gtmp_b, gr_b], wr=[gtmp_b])
            em.do("vector", lambda e, c=c: e.tensor_mul(c, gtmp[:], alpha_sb[:]),
                  rd=[gtmp_b, alpha_b], wr=[ct_b], tag="ctok")


            em.phase = "p3"

        def emit_routing():
            em.phase = "p12c"
            mtb_b, gc_b, off_b, grow_b = Bf("mtopb"), Bf("gcum"), Bf("offr"), Bf("grow")
            mer_b, gx_b, gxi_b, tkd_b = Bf("mergec"), Bf("gidx"), Bf("gidxi"), Bf("tkii")
            tki_b = Bf("tki")
            em.do("vector", lambda e: e.tensor_copy(mtopb[:], mtop[:]), rd=[mt_b], wr=[mtb_b])
            em.do("tensor", lambda e: e.matmul(PS[6][:, 0:64], TRI, mtopb[:], start=True, stop=True),
                  rd=[mtb_b, c_b], wr=[pbuf[6]], tag="cum_mm")
            em.do("vector", lambda e: e.tensor_copy(gcum[:], PS[6][:, 0:64]), rd=[pbuf[6]], wr=[gc_b])
            em.do("tensor", lambda e: e.matmul(PS[7][0:1, 0:64], ONESC, mtopb[:], start=True, stop=True),
                  rd=[mtb_b, c_b], wr=[pbuf[7]], tag="tot_mm")
            em.do("scalar", lambda e: e.copy(grow[0:1, 0:64], PS[7][0:1, 0:64]), rd=[pbuf[7], eps_b], wr=[grow_b])
            em.do("vector", lambda e: e.memset(grow[0:1, 64:72], 0.0), rd=[grow_b], wr=[grow_b])
            for tb in range(1, 8):
                em.do("vector", lambda e, tb=tb: e.tensor_add(
                    grow[0:1, 64 + tb * 8:72 + tb * 8], grow[0:1, 56 + tb * 8:64 + tb * 8],
                    grow[0:1, (tb - 1) * 8:tb * 8]), rd=[grow_b], wr=[grow_b], tag="off_acc")
            em.do("tensor", lambda e: e.matmul(PS[5][:, 0:64], ONESRF, grow[0:1, 64:128], start=True, stop=True),
                  rd=[grow_b, cf_b], wr=[pbuf[5]], tag="off_mm")
            em.do("vector", lambda e: e.tensor_copy(offr[:], PS[5][:, 0:64]), rd=[pbuf[5]], wr=[off_b])
            em.do("vector", lambda e: e.tensor_add(gcum[:], gcum[:], offr[:]), rd=[off_b, gc_b], wr=[gc_b])
            em.do("vector", lambda e: e.tensor_add(gcum[:], gcum[:], eCm1c[:]), rd=[rt_c, gc_b], wr=[gc_b])
            em.do("vector", lambda e: e.tensor_tensor(offr[:], gcum[:], bndc[:], ALU.is_ge),
                  rd=[rt_c, gc_b], wr=[off_b], tag="ovf")
            em.do("vector", lambda e: e.scalar_tensor_tensor(gcum[:], offr[:], 524288.0, gcum[:],
                                                             ALU.mult, ALU.add),
                  rd=[off_b, gc_b], wr=[gc_b], tag="ovf_push")
            for tb in range(8):
                gvs = gcum[:, tb * NE:(tb + 1) * NE]
                cs = ctok[:, tb * NE:(tb + 1) * NE]
                for k_, mb_, mv in ((0, m1_b, m1b), (1, m2_b, m2b)):
                    mvs = mv[:, tb * NE:(tb + 1) * NE]
                    em.do("vector", lambda e, mvs=mvs, gvs=gvs: e.tensor_mul(gmask[:], mvs, gvs),
                          rd=[mb_, gc_b], wr=[gm_b])
                    em.do("vector", lambda e, tb=tb, k_=k_: e.reduce_sum(
                        gidx[:, tb * 2 + k_:tb * 2 + k_ + 1], gmask[:], axis=AXX), rd=[gm_b], wr=[gx_b])
                    em.do("vector", lambda e, mvs=mvs, cs=cs: e.tensor_mul(gmask[:], mvs, cs),
                          rd=[mb_, ct_b], wr=[gm_b])
                    em.do("vector", lambda e, tb=tb, k_=k_: e.reduce_sum(
                        mergec[:, tb * 2 + k_:tb * 2 + k_ + 1], gmask[:], axis=AXX), rd=[gm_b], wr=[mer_b])
            em.do("vector", lambda e: e.tensor_scalar(
                offr[:, 32:48], gidx[:], float(NE * CAP - 1), None, ALU.min),
                rd=[gx_b], wr=[off_b], tag="gidx_clamp")
            em.do("vector", lambda e: e.tensor_copy(gidxi[:], offr[:, 32:48]), rd=[off_b], wr=[gxi_b], tag="gidx_cast")
            sg_b, sgi_b = Bf("sgix"), Bf("sgixi")
            em.do("vector", lambda e: e.tensor_scalar(offr[:, 0:16], gidx[:], 1.0 / 128.0, -0.499,
                                                      ALU.mult, ALU.add),
                  rd=[gx_b], wr=[off_b], tag="sg0")
            em.do("vector", lambda e: e.tensor_copy(sgixi[:], offr[:, 0:16]), rd=[off_b], wr=[sgi_b], tag="sg1")
            em.do("vector", lambda e: e.tensor_copy(sgix[:], sgixi[:]), rd=[sgi_b], wr=[sg_b], tag="sg2")
            em.do("vector", lambda e: e.scalar_tensor_tensor(
                offr[:, 16:32], sgix[:], -128.0, gidx[:], ALU.mult, ALU.add),
                rd=[sg_b, gx_b], wr=[off_b], tag="sg3")
            em.do("vector", lambda e: e.scalar_tensor_tensor(
                sgix[:], offr[:, 16:32], float(NE * NSB) + 0.0, sgix[:], ALU.mult, ALU.add),
                rd=[off_b, sg_b], wr=[sg_b], tag="sg4")
            em.do("vector", lambda e: e.tensor_copy(sgixi[:], sgix[:]), rd=[sg_b], wr=[sgi_b], tag="sg5")
            for tb in range(8):
                em.do("vector", lambda e, tb=tb: e.tensor_scalar_add(
                    tkid[:, tb:tb + 1], i128c[:, 0:1], float(128 * tb)), rd=[rt_c], wr=[Bf("tkid")], tag="tkid")
            tkd_bb = Bf("tkid")
            for tb in range(8):
                for k_ in range(2):
                    em.do("gpsimd", lambda e, tb=tb, k_=k_: e.indirect_dma_start(
                        out=d_toki.ap(),
                        out_offset=bass.IndirectOffsetOnAxis(ap=sgixi[:, tb * 2 + k_:tb * 2 + k_ + 1], axis=0),
                        in_=tkid[:, tb:tb + 1], in_offset=None, oob_is_err=False),
                        rd=[sgi_b, tkd_bb], wr=[toki_b], dma=True, tag="toki_scat")
            em.do("sync", lambda e: e.dma_start(
                out=tki[:], in_=d_toki.ap()[0:NE * CAP, :].rearrange("(p c) one -> p (c one)", p=128)),
                rd=[toki_b], wr=[tki_b], dma=True, tag="tki_ld")
            em.do("vector", lambda e: e.tensor_copy(tkii[:], tki[:]), rd=[tki_b], wr=[tkd_b], tag="tki_cast")
            for nm, tt, bb in (("tki", tki, tki_b), ("gidx", gidx, gx_b), ("mergec", mergec, mer_b),
                               ("mtop", mtop, mt_b), ("gcum", gcum, gc_b), ("gtok", gtok, gt_b),
                               ("ctok", ctok, ct_b), ("grow", grow, grow_b)):
                t_ = dbg(nm, list(tt.shape), F32)
                if t_ is not None:
                    em.do("sync", lambda e, t_=t_, tt=tt: e.dma_start(out=t_.ap(), in_=tt[:]),
                          rd=[bb], wr=[Bf("dbgout")], dma=True)

            em.phase = "p3"

        fc1sl_b = [Bf(f"fc1sl{i}") for i in range(4)]
        fc2sl_b = [Bf(f"fc2sl{i}") for i in range(2)]

        def emit_prefetch():
            em.overlay(fc1sl_b, [xch_b[0], xch_b[1]], tag="ovl_fc1pre")
            for ib in range(4):
                em.do("gpsimd", lambda e, ib=ib: e.dma_start(
                    out=fc1sl[ib][:], in_=d_fc1.ap()[0, ib].rearrange("p jc i -> p (jc i)")),
                    wr=[fc1sl_b[ib]], dma=True, tag="fc1_load")

        for nm, tt, bb in (("qF", qF, qF_b), ("kF", kF, kF_b), ("vTok", vTok, vT_b)):
            t_ = dbg(nm, [128, 16384], BF16)
            if t_ is not None:
                em.do("sync", lambda e, t_=t_, tt=tt: e.dma_start(out=t_.ap(), in_=tt[:]), rd=[bb], wr=[Bf("dbgout")], dma=True)

        tkd_b, gxi_b, mer_b = Bf("tkii"), Bf("gidxi"), Bf("mergec")
        ct_b = Bf("ctok")

        em.phase = "p3"
        e_b = [Bf(f"etile{i}") for i in range(4)]
        es_b = [Bf(f"esum{i}") for i in range(2)]
        rrow_b, rrowb_b, rrep_b = Bf("rrow"), Bf("rrowb"), Bf("rrep")
        t2_b, u_b = Bf("t2sb"), Bf("usb")
        em.overlay([yF_b, rrep_b, t2_b, u_b] + e_b + es_b, [wqkv_b], tag="ovl_p3")
        em.overlay([rrow_b, rrowb_b], [wsb_b], tag="ovl_rrow")

        ecnt = [0]
        SCB = (2, 3, 5)
        ONESF = epsc[:, 3:4]

        def attn_scores(h, var, qc, pa0, pa1):
            chb = h * 2 + var
            nkb = 4 * qc + 4
            es, esb = esum[var], es_b[var]
            tiles = []

            def emit_pv(kb):
                et, etb, N, start_l = tiles[kb]
                for half in range(2):
                    pa = pa0 if half == 0 else pa1
                    vl = vTok[:, kb * 512 + h * 256 + half * 128: kb * 512 + h * 256 + (half + 1) * 128]
                    em.do("tensor", lambda e, vl=vl, et=et, pa=pa, N=N, start_l=start_l, kb=kb, nkb=nkb:
                          e.matmul(PS[pa][:, start_l:512], vl, et[:, 0:N],
                                   start=(kb == 0), stop=(kb == nkb - 1)),
                          rd=[vT_b, etb], wr=[pbuf[pa]], tag="pv_mm")

            for kb in range(nkb):
                start_l = max(0, 128 * kb - 512 * qc)
                N = 512 - start_l
                ps_s = SCB[kb % 3]
                ei = ecnt[0] % 4
                ecnt[0] += 1
                et, etb = etile[ei], e_b[ei]
                tiles.append((et, etb, N, start_l))
                lhs = kF[:, chb * T + kb * 128: chb * T + (kb + 1) * 128]
                rhs = qF[:, chb * T + qc * 512 + start_l: chb * T + (qc + 1) * 512]
                em.do("tensor", lambda e, lhs=lhs, rhs=rhs, ps_s=ps_s, N=N: e.matmul(
                    PS[ps_s][:, 0:N], lhs, rhs, start=True, stop=True),
                    rd=[kF_b, qF_b], wr=[pbuf[ps_s]], tag="score_mm")
                em.do("scalar", lambda e, ps_s=ps_s, et=et, N=N: e.activation(
                    et[:, 0:N], PS[ps_s][:, 0:N], AF.Exp, scale=SCALE),
                    rd=[pbuf[ps_s], eps_b], wr=[etb], tag="exp")
                if kb >= 4 * qc:
                    em.do("gpsimd", lambda e, et=et: e.tensor_mul(et[:, 0:128], et[:, 0:128], TRI),
                          rd=[etb, c_b], wr=[etb], tag="diag_mask")
                if kb == 0:
                    em.do("vector", lambda e, et=et, es=es: e.tensor_copy(es[:], et[:]),
                          rd=[etb], wr=[esb], tag="colsum0")
                else:
                    em.do("vector", lambda e, et=et, es=es, N=N, start_l=start_l:
                          e.tensor_add(es[:, start_l:512], es[:, start_l:512], et[:, 0:N]),
                          rd=[etb, esb], wr=[esb], tag="colsum")
                if kb >= 2:
                    emit_pv(kb - 2)
            emit_pv(nkb - 2)
            emit_pv(nkb - 1)

            def finish():
                em.do("tensor", lambda e, es=es: e.matmul(
                    PS[4][0:1, 0:512], ONESF, es[:], start=True, stop=True),
                    rd=[esb, eps_b], wr=[pbuf[4]], tag="colred_mm")
                em.do("vector", lambda e: e.reciprocal(rrow[:], PS[4][0:1, 0:512]),
                      rd=[pbuf[4]], wr=[rrow_b], tag="recip")
                em.do("vector", lambda e: e.tensor_copy(rrowb[:], rrow[:]), rd=[rrow_b], wr=[rrowb_b])
                em.do("tensor", lambda e: e.matmul(PS[4][:], ONESR, rrowb[:], start=True, stop=True),
                      rd=[rrowb_b, c_b], wr=[pbuf[4]], tag="rrep_mm")
                em.do("scalar", lambda e: e.copy(rrep[:], PS[4][:]), rd=[pbuf[4], eps_b], wr=[rrep_b], tag="rrep_ev")
            return finish

        for h in range(2):
            for qc in range(8):
                gi = h * 8 + qc
                fin1 = attn_scores(h, 1, qc, 0, 1)
                fin0 = attn_scores(h, 0, qc, 6, 7)
                fin1()
                for half in range(2):
                    em.do("vector", lambda e, half=half: e.tensor_mul(
                        t2sb[:, half * 512:(half + 1) * 512], PS[half][:], rrep[:]),
                        rd=[pbuf[half], rrep_b], wr=[t2_b], tag="t2")
                fin0()
                for half in range(2):
                    pa = 6 if half == 0 else 7
                    em.do("vector", lambda e, pa=pa, half=half: e.tensor_mul(usb[:], PS[pa][:], rrep[:]),
                          rd=[pbuf[pa], rrep_b], wr=[u_b], tag="u")
                    yv = yF[:, (h * 2 + half) * T + qc * 512:(h * 2 + half) * T + (qc + 1) * 512]
                    em.do("vector", lambda e, yv=yv, half=half: e.scalar_tensor_tensor(
                        yv, t2sb[:, half * 512:(half + 1) * 512], nlam_col[:, 0:1], usb[:],
                        ALU.mult, ALU.add), rd=[t2_b, u_b, nlam_b], wr=[yF_b], tag="ycomb")
                if gi in (1, 3, 5, 7):
                    emit_slice_chunk((gi - 1) // 2)
                elif gi == 8:
                    emit_routing()
                elif gi == 9:
                    emit_prefetch()

        t_ = dbg("yF", [128, 4 * T], BF16)
        if t_ is not None:
            em.do("sync", lambda e, t_=t_: e.dma_start(out=t_.ap(), in_=yF[:]), rd=[yF_b], wr=[Bf("dbgout")], dma=True)

        em.phase = "p4"
        wo_b = [Bf(f"wosl{i}") for i in range(4)]
        po_b = [Bf(f"pout8_{i}") for i in range(16)]
        xg_b = Bf("xg_tm")
        xge_b = [Bf("xge0"), Bf("xge1")]
        em.overlay(wo_b, [rrep_b, t2_b, u_b] + e_b + es_b, tag="ovl_p4")
        em.overlay(po_b, [vT_b], tag="ovl_pout")
        em.overlay([xg_b], [qF_b], tag="ovl_xg")
        em.overlay(xge_b, [qF_b, kF_b], tag="ovl_xge")
        pfq_b = [Bf(f"p_full{q}") for q in range(4)]

        def emit_head(e_, part):
            par = e_ % 2
            if part == 0:
                for sblk in range(NSB):
                    em.do("gpsimd", lambda e, e_=e_, sblk=sblk: e.indirect_dma_start(
                        out=xg_tm[:, sblk * D:(sblk + 1) * D], out_offset=None,
                        in_=d_xtm.ap(),
                        in_offset=bass.IndirectOffsetOnAxis(
                            ap=tkii[:, e_ * NSB + sblk:e_ * NSB + sblk + 1], axis=0),
                        oob_is_err=False),
                        rd=[tkd_b, xtmd_b], wr=[xg_b], dma=True, tag="xg_gather")
                return
            for jc in range((part - 1) * 8, part * 8):
                pp = 6 + (jc % 2)
                for sblk in range(NSB):
                    src = xg_tm[:, sblk * D + jc * 128: sblk * D + (jc + 1) * 128]
                    em.do("tensor", lambda e, src=src, pp=pp, sblk=sblk: e.transpose(
                        PS[pp][:, sblk * 128:(sblk + 1) * 128], src, IDNF),
                        rd=[xg_b, cf_b], wr=[pbuf[pp]], tag="xg_tr")
                em.do("vector", lambda e, pp=pp, jc=jc, par=par: e.tensor_copy(
                    xge[par][:, jc * CAP:(jc + 1) * CAP], PS[pp][:, 0:CAP]),
                    rd=[pbuf[pp]], wr=[xge_b[par]], tag="xg_tr_ev")
        rsq_b = [Bf(f"rs_out{q}") for q in range(4)]
        for cc_ in range(4):
            em.do("gpsimd", lambda e, cc_=cc_: e.dma_start(
                out=wosl[cc_][:], in_=d_woT.ap()[cc_ * 128:(cc_ + 1) * 128, :]),
                wr=[wo_b[cc_]], dma=True, tag="wo_load")
        for q in range(4):
            for a in range(4):
                for j2 in range(2):
                    tb = 8 * a + 2 * q + j2
                    bk0 = 0
                    rg0 = (tb % 4) * 4
                    for cc_ in range(4):
                        lhs = yF[:].rearrange("p (c t) -> p c t", c=4)[:, cc_, tb * 128:(tb + 1) * 128]
                        for nb in range(4):
                            em.do("tensor", lambda e, lhs=lhs, cc_=cc_, nb=nb, bk0=bk0: e.matmul(
                                PS[bk0 + nb][:], lhs, wosl[cc_][:, nb * 512:(nb + 1) * 512],
                                start=(cc_ == 0), stop=(cc_ == 3)),
                                rd=[yF_b, wo_b[cc_]], wr=[pbuf[bk0 + nb]], tag="wo_mm")
                    r0 = 256 * a + 128 * j2
                    for nb in range(4):
                        ridx = rg0 + nb
                        bki = bk0 + nb
                        pv = pout8[:, ridx * 512:(ridx + 1) * 512]
                        em.do("scalar", lambda e, bki=bki, pv=pv: e.activation(
                            pv, PS[bki][:], AF.Copy, scale=1.0 - LAM0),
                            rd=[pbuf[bki], eps_b], wr=[po_b[ridx]], tag="wo_evac")
                        em.do(("sync", "gpsimd")[nb % 2],
                              lambda e, q=q, r0=r0, nb=nb, pv=pv: e.dma_start(
                                  out=d_pfq[q].ap()[r0:r0 + 128, nb * 512:(nb + 1) * 512], in_=pv),
                              rd=[po_b[ridx]], wr=[pfq_b[q]], dma=True, tag="p_store")
            if q < 2:
                em.do("gpsimd", lambda e, q=q: e.collective_compute(
                    "ReduceScatter", ALU.add, replica_groups=[[0, 1, 2, 3], [4, 5, 6, 7]],
                    ins=[d_pfq[q].ap()], outs=[d_rsq[q].ap()]),
                    rd=[pfq_b[q]], wr=[rsq_b[q]], cc=True, tag=f"rs{q}")
            if q < 3:
                emit_head(0, q)

        em.phase = "p52"
        h_b = [Bf("hsb0"), Bf("hsb1")]
        y2_b = [Bf("y2sb0"), Bf("y2sb1")]
        gelu_b = Bf("gelu_sc")
        y2d_b = Bf("y2e")
        em.overlay(y2_b, [kF_b] + po_b, tag="ovl_y2")
        em.overlay(h_b, [yF_b], tag="ovl_h")
        em.overlay(fc2sl_b, wo_b, tag="ovl_fc2b")

        for e_ in range(NE):
            par = e_ % 2
            if e_ == 1:
                for q in (2, 3):
                    em.do("gpsimd", lambda e, q=q: e.collective_compute(
                        "ReduceScatter", ALU.add, replica_groups=[[0, 1, 2, 3], [4, 5, 6, 7]],
                        ins=[d_pfq[q].ap()], outs=[d_rsq[q].ap()]),
                        rd=[pfq_b[q]], wr=[rsq_b[q]], cc=True, tag=f"rs{q}")
            if e_ != 0:
                emit_head(e_, 0)
                emit_head(e_, 1)
                emit_head(e_, 2)
            for ib in range(16):
                sl = ib % 4
                if not (e_ == 0 and ib < 4):
                    em.do("gpsimd", lambda e, e_=e_, ib=ib, sl=sl: e.dma_start(
                        out=fc1sl[sl][:], in_=d_fc1.ap()[e_, ib].rearrange("p jc i -> p (jc i)")),
                        wr=[fc1sl_b[sl]], dma=True, tag="fc1_load")
                for jc in range(16):
                    lhs = fc1sl[sl][:, jc * 128:(jc + 1) * 128]
                    rhs = xge[par][:, jc * CAP:(jc + 1) * CAP]
                    em.do("tensor", lambda e, lhs=lhs, rhs=rhs, ps_i=4 + ib % 2, jc=jc: e.matmul(
                        PS[ps_i][:, 0:CAP], lhs, rhs, start=(jc == 0), stop=(jc == 15)),
                        rd=[fc1sl_b[sl], xge_b[par]], wr=[pbuf[4 + ib % 2]], tag="fc1_mm")
                bcol = fc1b_sb[:, e_ * 16 + ib: e_ * 16 + ib + 1]
                hv = hsb[par][:, ib * CAP:(ib + 1) * CAP]
                em.do("scalar", lambda e, ps_i=4 + ib % 2, bcol=bcol, hv=hv: e.activation(
                    hv, PS[ps_i][:, 0:CAP], AF.Gelu, bias=bcol),
                    rd=[pbuf[4 + ib % 2], fc1b_b], wr=[h_b[par]], tag="gelu")
            for nb in range(4):
                for ibh in range(2):
                    sl = ibh
                    if True:
                        em.do("gpsimd", lambda e, e_=e_, nb=nb, ibh=ibh, sl=sl: e.dma_start(
                            out=fc2sl[sl][:],
                            in_=d_fc2.ap()[e_, nb, ibh].rearrange("p ib c -> p (ib c)")),
                            wr=[fc2sl_b[sl]], dma=True, tag="fc2_load")
                    for ib8 in range(8):
                        ib = ibh * 8 + ib8
                        for tb in range(NSB):
                            lhs = hsb[par][:, ib * CAP + tb * 128: ib * CAP + (tb + 1) * 128]
                            em.do("tensor", lambda e, lhs=lhs, sl=sl, ib8=ib8, tb=tb,
                                  first=(ib == 0): e.matmul(
                                      PS[tb][:], lhs, fc2sl[sl][:, ib8 * 512:(ib8 + 1) * 512],
                                      start=first, stop=False),
                                rd=[h_b[par], fc2sl_b[sl]], wr=[pbuf[tb]], tag="fc2_mm")
                for tb in range(NSB):
                    em.do("tensor", lambda e, e_=e_, tb=tb, nb=nb: e.matmul(
                        PS[tb][:], ind_sb[0:8, e_ * 128:(e_ + 1) * 128],
                        fc2b_sb[0:8, nb * 512:(nb + 1) * 512], start=False, stop=True),
                        rd=[ind_b, fc2b_b], wr=[pbuf[tb]], tag="b2_mm")
                for tb in range(NSB):
                    yv = y2sb[par][:, tb * D + nb * 512: tb * D + (nb + 1) * 512]
                    em.do("scalar", lambda e, tb=tb, yv=yv: e.copy(yv, PS[tb][:]),
                          rd=[pbuf[tb], eps_b], wr=[y2_b[par]], tag="y2_ev")
            for tb in range(NSB):
                em.do("sync", lambda e, e_=e_, tb=tb, par=par: e.dma_start(
                    out=d_y2e.ap()[e_ * CAP + tb * 128: e_ * CAP + (tb + 1) * 128, :],
                    in_=y2sb[par][:, tb * D:(tb + 1) * D]),
                    rd=[y2_b[par]], wr=[y2d_b], dma=True, tag="y2_store")

        em.phase = "p6"
        xr_b = [Bf("xr0"), Bf("xr1")]
        rt_b = [Bf("rt0"), Bf("rt1")]
        mg_b = [Bf(f"mgt{i}") for i in range(4)]
        em.overlay(xr_b + rt_b, h_b, tag="ovl_p6")
        em.overlay(mg_b, [vT_b] + y2_b + po_b, tag="ovl_mg")
        out_b = Bf("out_done")
        for tb in range(8):
            sl = tb % 2
            q, row0 = tb // 2, 128 * (tb % 2)
            for k_ in range(2):
                mi = k_ * 2 + sl
                em.do("gpsimd", lambda e, tb=tb, k_=k_, mi=mi: e.indirect_dma_start(
                    out=mgt[mi][:], out_offset=None, in_=d_y2e.ap(),
                    in_offset=bass.IndirectOffsetOnAxis(
                        ap=gidxi[:, tb * 2 + k_:tb * 2 + k_ + 1], axis=0),
                    oob_is_err=False),
                    rd=[gxi_b, y2d_b], wr=[mg_b[mi]], dma=True, tag="mg_gather")
            em.do("sync", lambda e, q=q, row0=row0, sl=sl: e.dma_start(
                out=rtile[sl][:], in_=d_rsq[q].ap()[row0:row0 + 128, :]),
                rd=[rsq_b[q]], wr=[rt_b[sl]], dma=True, tag="rs_load")
            em.do("sync", lambda e, tb=tb, sl=sl: e.dma_start(
                out=xrtile[sl][:], in_=d_xres.ap()[tb * 128:(tb + 1) * 128, :]),
                wr=[xr_b[sl]], dma=True, tag="xres_load")
            for k_ in range(2):
                mi = k_ * 2 + sl
                em.do("vector", lambda e, tb=tb, k_=k_, mi=mi, sl=sl: e.scalar_tensor_tensor(
                    xrtile[sl][:], mgt[mi][:], mergec[:, tb * 2 + k_:tb * 2 + k_ + 1],
                    xrtile[sl][:], ALU.mult, ALU.add),
                    rd=[mg_b[mi], mer_b, xr_b[sl]], wr=[xr_b[sl]], tag="merge")
            em.do("vector", lambda e, sl=sl: e.tensor_add(xrtile[sl][:], xrtile[sl][:], rtile[sl][:]),
                  rd=[rt_b[sl], xr_b[sl]], wr=[xr_b[sl]], tag="rs_add")
            em.do("sync", lambda e, tb=tb, sl=sl: e.dma_start(
                out=d_out.ap()[tb * 128:(tb + 1) * 128, :], in_=xrtile[sl][:]),
                rd=[xr_b[sl]], wr=[out_b], dma=True, tag="out_store")

        em.fence("scalar", [out_b], tag="final")
        em.final_fence("scalar")
        sem_of = {}
        for i, k in enumerate(sorted(em.all_sem_keys(), key=str)):
            sem_of[k] = en(nc.semaphore(f"S{i}"))
        em.finalize(block, sem_of)

    return nc, list(dbg_tensors.keys())



def _pm(a):
    return np.ascontiguousarray(a.reshape(16, 128, -1).transpose(1, 0, 2)).astype(np.float32)


def stage_inputs(inputs):
    x = np.asarray(inputs["L0_attn_x"], dtype=np.float32)
    wq = np.asarray(inputs["L0_attn_wq"], dtype=np.float32)
    wk = np.asarray(inputs["L0_attn_wk"], dtype=np.float32)
    wv = np.asarray(inputs["L0_attn_wv"], dtype=np.float32)
    wo = np.asarray(inputs["L0_attn_wo"], dtype=np.float32)
    lq1 = np.asarray(inputs["L0_attn_lq1"], dtype=np.float32)
    lk1 = np.asarray(inputs["L0_attn_lk1"], dtype=np.float32)
    lq2 = np.asarray(inputs["L0_attn_lq2"], dtype=np.float32)
    lk2 = np.asarray(inputs["L0_attn_lk2"], dtype=np.float32)
    gate_w = np.asarray(inputs["L0_ffn_gate_w"], dtype=np.float32)
    fc1_w = np.asarray(inputs["L0_ffn_fc1_w"], dtype=np.float32)
    fc1_b = np.asarray(inputs["L0_ffn_fc1_b"], dtype=np.float32)
    fc2_w = np.asarray(inputs["L0_ffn_fc2_w"], dtype=np.float32)
    fc2_b = np.asarray(inputs["L0_ffn_fc2_b"], dtype=np.float32)
    alpha = np.asarray(inputs["L0_ffn_alpha"], dtype=np.float32)

    fc1_s = np.ascontiguousarray(
        fc1_w.reshape(NE, 16, 128, 16, 128).transpose(0, 3, 2, 1, 4)).astype(np.float32)
    fc2_s = np.ascontiguousarray(
        fc2_w.reshape(NE, 2, 8, 128, 4, 512).transpose(0, 4, 1, 3, 2, 5))
    fc1b_s = np.ascontiguousarray(fc1_b.reshape(NE, 16, 128).transpose(2, 0, 1).reshape(128, NE * 16))
    fc2b_s = np.ascontiguousarray(fc2_b)
    gate_s = np.ascontiguousarray(gate_w.reshape(16, 128, NE).transpose(1, 0, 2))
    alpha_s = np.ascontiguousarray(np.broadcast_to(alpha[None, :], (128, NE)))
    lqk = np.ascontiguousarray(np.concatenate([lq1, lk1, lq2, lk2])[None, :].astype(np.float32))

    inv_freq = (1.0 / 10000.0) ** (np.arange(0, 128, 2, dtype=np.float32) / 128.0)

    in_maps = []
    for c in range(NCORES):
        b, r = c // 4, c % 4
        hpair = (2 * r, 2 * r + 1)
        xT = _pm(x[b].T)
        xresT = np.ascontiguousarray(xT[:, :, r * TSL:(r + 1) * TSL])
        xres = np.ascontiguousarray(x[b, r * TSL:(r + 1) * TSL, :])
        cs = slice(512 * r, 512 * r + 512)
        wqkvT = _pm(np.concatenate([wq.T[:, cs], wk.T[:, cs], wv.T[:, cs]], axis=1))
        woT = np.ascontiguousarray(wo.T[cs, :])
        cosr = np.zeros((128, 4, 64), dtype=np.float32)
        sinr = np.zeros((128, 4, 64), dtype=np.float32)
        for bi, hh in enumerate((hpair[0], hpair[0], hpair[1], hpair[1])):
            ang = hh * inv_freq
            cosr[:, bi, :] = np.cos(ang)[None, :]
            sinr[:, bi, :] = np.sin(ang)[None, :]
        ropes = np.ascontiguousarray(np.stack([cosr, sinr], axis=1))
        in_maps.append({
            "xT": xT, "xresT": xresT, "xres": xres, "wqkvT": wqkvT, "woT": woT,
            "ropes": ropes, "lqk": lqk, "gate_s": gate_s, "alpha_s": alpha_s,
            "fc1_s": fc1_s, "fc1b_s": fc1b_s, "fc2_s": fc2_s, "fc2b_s": fc2b_s,
        })
    return in_maps


def kernel(**inputs):
    nc, dbg_names = build_program(debug=tuple(DEBUG_DUMPS))
    in_maps = stage_inputs(inputs)
    res = run_bass_kernel_spmd(nc, in_maps, core_ids=list(range(NCORES)))
    outs = [np.asarray(res.results[c]["out"]) for c in range(NCORES)]
    full = np.stack(outs, axis=0).reshape(B, T, D).astype(np.float32)
    if dbg_names:
        kernel._dbg = [{n: np.asarray(res.results[c]["dbg_" + n]) for n in dbg_names}
                       for c in range(NCORES)]
    return full


# version 22
# speedup vs baseline: 1.0037x; 1.0037x over previous
import numpy as np
import ml_dtypes
import concourse.bass as bass
import concourse.mybir as mybir
from concourse.bass_utils import run_bass_kernel_spmd

F32 = mybir.dt.float32
BF16 = mybir.dt.bfloat16
AF = mybir.ActivationFunctionType
ALU = mybir.AluOpType
AXX = mybir.AxisListType.X

B, T, D = 2, 4096, 2048
TSL = 1024
NCORES = 8
SCALE = 1.0 / float(np.sqrt(128.0))
LAM0 = 0.2
RMS_EPS = 1.1920929e-07
LN_EPS = 1e-5
NE = 8
HH = 128
NT = 256
CAP = 384
NSB = CAP // 128

DEBUG_DUMPS = []



class _Op:
    __slots__ = ("eng", "fn", "deps", "inc", "attach_inc", "needed", "tgt", "sem_key", "tag", "phase")

    def __init__(self, eng, fn, deps, inc, tag, attach_inc=None):
        self.eng, self.fn, self.deps, self.inc, self.tag = eng, fn, deps, inc, tag
        self.attach_inc = attach_inc if attach_inc is not None else inc
        self.needed = False
        self.tgt = None
        self.sem_key = None
        self.phase = None


class Buf:
    def __init__(self, name=""):
        self.name = name
        self.w = None
        self.rs = []

    def rd_deps(self):
        return [self.w] if self.w is not None else []

    def wr_deps(self):
        d = [self.w] if self.w is not None else []
        return d + self.rs


class Em:
    ENGINES = ("sync", "scalar", "vector", "tensor", "gpsimd")

    def __init__(self, nc):
        self.nc = nc
        self.ops = []
        self.phase = "p0"

    def do(self, eng, fn, rd=(), wr=(), dma=False, cc=False, tag="", inc=None):
        deps, seen = [], set()
        for b in rd:
            for d in b.rd_deps():
                if id(d) not in seen:
                    seen.add(id(d))
                    deps.append(d)
        for b in wr:
            for d in b.wr_deps():
                if id(d) not in seen:
                    seen.add(id(d))
                    deps.append(d)
        op = _Op(eng, fn, deps, inc if inc is not None else (16 if dma else 1), tag,
                 attach_inc=(16 if dma else 1))
        op.phase = self.phase
        if cc:
            op.sem_key = ("CC", tag)
        elif dma:
            assert wr, f"dma op {tag} needs a target buf"
            op.sem_key = ("D", wr[0].name, "g" if eng == "gpsimd" else "h")
        else:
            op.sem_key = (eng, "cmp")
        self.ops.append(op)
        for b in rd:
            b.rs.append(op)
        for b in wr:
            b.w = op
            b.rs = []
        return op

    fence_fn = None
    fence_deps = ()

    def fence(self, eng, bufs, tag="fence"):
        deps, seen = [], set()
        for b in list(bufs) + list(self.fence_deps):
            for d in b.wr_deps():
                if id(d) not in seen:
                    seen.add(id(d))
                    deps.append(d)
        op = _Op(eng, self.fence_fn or (lambda e: e.nop(nofuse=True)), deps, 1, tag)
        op.sem_key = (eng, "cmp")
        self.ops.append(op)
        for b in self.fence_deps:
            b.w = op
            b.rs = []
        return op

    def overlay(self, new_bufs, old_bufs, tag="ovl"):
        op = self.fence("scalar", old_bufs, tag=tag)
        for b in new_bufs:
            b.w = op
            b.rs = []

    def final_fence(self, eng="scalar"):
        last = {}
        for op in self.ops:
            if op.sem_key[0] in ("D", "CC"):
                last[op.sem_key] = op
        deps = list(last.values())
        for b in self.fence_deps:
            deps.extend(b.wr_deps())
        op = _Op(eng, self.fence_fn or (lambda e: e.nop(nofuse=True)), deps, 1, "final")
        op.sem_key = (eng, "cmp")
        self.ops.append(op)
        for b in self.fence_deps:
            b.w = op
            b.rs = []
        return op

    def finalize(self, block, sem_of):
        for op in self.ops:
            op.deps = [d for d in op.deps
                       if not (d.eng == "tensor" and op.eng == "tensor")]
            for d in op.deps:
                d.needed = True
            if op.sem_key[0] in ("D", "CC"):
                op.needed = True
        counters = {}
        for op in self.ops:
            if op.needed:
                c = counters.get(op.sem_key, 0) + op.inc
                counters[op.sem_key] = c
                op.tgt = c
        per_eng = {e: [] for e in self.ENGINES}
        for op in self.ops:
            per_eng[op.eng].append(op)

        def body_for(eng):
            ops = per_eng[eng]

            def body(e):
                watermark = {}
                cur_phase = [None]
                from contextlib import ExitStack as _ES
                scope_stack = _ES()
                for op in ops:
                    if op.phase != cur_phase[0]:
                        scope_stack.close()
                        scope_stack = _ES()
                        if op.phase:
                            scope_stack.enter_context(self.nc.named_scope(op.phase))
                        cur_phase[0] = op.phase
                    waits = {}
                    for d in op.deps:
                        k = d.sem_key
                        if d.tgt > waits.get(k, 0):
                            waits[k] = d.tgt
                    for k in sorted(waits, key=str):
                        v = waits[k]
                        if watermark.get(k, 0) >= v:
                            continue
                        e.wait_ge(sem_of[k], v)
                        watermark[k] = v
                    try:
                        inst = op.fn(e)
                    except Exception as ex:
                        raise RuntimeError(f"emit failed: eng={op.eng} tag={op.tag} phase={op.phase}") from ex
                    if op.needed:
                        inst.then_inc(sem_of[op.sem_key], op.attach_inc)
                scope_stack.close()
            return body

        for eng in self.ENGINES:
            if per_eng[eng]:
                getattr(block, eng)(body_for(eng))

    def all_sem_keys(self):
        return {op.sem_key for op in self.ops}



def build_program(debug=()):
    nc = bass.Bass("TRN2", target_bir_lowering=False)
    em = Em(nc)

    d_xT = nc.dram_tensor("xT", [128, 16, T], F32, kind="ExternalInput")
    d_xresT = nc.dram_tensor("xresT", [128, 16, TSL], F32, kind="ExternalInput")
    d_xres = nc.dram_tensor("xres", [TSL, D], F32, kind="ExternalInput")
    d_wqkv = nc.dram_tensor("wqkvT", [128, 16, 1536], F32, kind="ExternalInput")
    d_woT = nc.dram_tensor("woT", [512, D], F32, kind="ExternalInput")
    d_rope = nc.dram_tensor("ropes", [128, 2, 4, 64], F32, kind="ExternalInput")
    d_lqk = nc.dram_tensor("lqk", [1, 1024], F32, kind="ExternalInput")
    d_gate = nc.dram_tensor("gate_s", [128, 16, NE], F32, kind="ExternalInput")
    d_alpha = nc.dram_tensor("alpha_s", [128, NE], F32, kind="ExternalInput")
    d_fc1 = nc.dram_tensor("fc1_s", [NE, 16, 128, 16, 128], F32, kind="ExternalInput")
    d_fc1b = nc.dram_tensor("fc1b_s", [128, NE * 16], F32, kind="ExternalInput")
    d_fc2 = nc.dram_tensor("fc2_s", [NE, 4, 2, 128, 8, 512], F32, kind="ExternalInput")
    d_fc2b = nc.dram_tensor("fc2b_s", [NE, D], F32, kind="ExternalInput")
    d_out = nc.dram_tensor("out", [TSL, D], F32, kind="ExternalOutput")

    d_pfq = [nc.dram_tensor(f"p_full{q}", [TSL, D], BF16, kind="Internal") for q in range(4)]
    d_rsq = [nc.dram_tensor(f"rs_out{q}", [256, D], BF16, kind="Internal") for q in range(4)]
    d_xtm = nc.dram_tensor("xn_tm", [TSL, D], F32, kind="Internal")
    d_toki = nc.dram_tensor("tok_i", [7296, 1], F32, kind="Internal")
    d_y2e = nc.dram_tensor("y2e", [NE * CAP, D], BF16, kind="Internal")

    tri = np.triu(np.ones((128, 128), dtype=np.float32)).astype(ml_dtypes.bfloat16)
    c_tri = nc.inline_tensor(tri, name="c_tri")
    c_id = nc.inline_tensor(np.eye(128, dtype=ml_dtypes.bfloat16), name="c_id")
    c_ones_r = nc.inline_tensor(np.ones((1, 128), dtype=ml_dtypes.bfloat16), name="c_ones_r")
    c_ones_rf = nc.inline_tensor(np.ones((1, 128), dtype=np.float32), name="c_ones_rf")
    c_idf = nc.inline_tensor(np.eye(128, dtype=np.float32), name="c_idf")
    c_mean_c = nc.inline_tensor(np.full((128, 1), 1.0 / D, dtype=ml_dtypes.bfloat16), name="c_mean_c")
    c_ones_c = nc.inline_tensor(np.ones((128, 1), dtype=ml_dtypes.bfloat16), name="c_ones_c")
    ind = np.zeros((8, NE * 128), dtype=np.float32)
    for e in range(NE):
        ind[e, e * 128:(e + 1) * 128] = 1.0
    c_ind = nc.inline_tensor(ind.astype(ml_dtypes.bfloat16), name="c_ind")
    c_fz = nc.inline_tensor(np.zeros((1, 4), dtype=np.float32), name="c_fz")
    c_eps = nc.inline_tensor(
        np.broadcast_to(np.array([LN_EPS, RMS_EPS, 0.0, 1.0], dtype=np.float32), (128, 4)).copy(),
        name="c_eps")
    c_iota8 = nc.inline_tensor(
        np.broadcast_to(np.arange(NE, dtype=np.float32), (128, NE)).copy(), name="c_iota8")
    ecol = np.tile(np.arange(NE, dtype=np.float32), 8)
    c_eCm1 = nc.inline_tensor(
        np.broadcast_to(ecol * CAP - 1.0, (128, 64)).copy(), name="c_eCm1")
    c_bnd = nc.inline_tensor(
        np.broadcast_to((ecol + 1.0) * CAP, (128, 64)).copy(), name="c_bnd")
    c_i128 = nc.inline_tensor(np.arange(128, dtype=np.float32)[:, None].copy(), name="c_i128")

    dbg_tensors = {}

    def dbg(name, shape, dtype=F32):
        if name in debug:
            dbg_tensors[name] = nc.dram_tensor("dbg_" + name, shape, dtype, kind="ExternalOutput")
            return dbg_tensors[name]
        return None

    ABS0 = (nc.sbuf_base + 31) & ~31
    _cursor = [ABS0 + 512]

    _perm_off = {}

    def perm(name, shape, dtype):
        nbytes = int(np.prod(shape[1:])) * mybir.dt.size(dtype)
        off = _cursor[0]
        _perm_off[name] = off
        _cursor[0] = (off + nbytes + 31) & ~31
        return nc.alloc_sbuf_tensor_at(name, shape, dtype, offset=off)

    def at(name, shape, dtype, off):
        return nc.alloc_sbuf_tensor_at(name, shape, dtype, offset=off)

    consts = perm("consts", [128, 392], BF16)
    constsf = perm("constsf", [128, 256], F32)
    ind_sb = perm("ind_sb", [8, NE * 128], BF16)
    lam_sb = perm("lam_sb", [1, 8], F32)
    nlam_col = perm("nlam_col", [128, 1], F32)
    ropes = perm("ropes_sb", [128, 512], BF16)
    lnrow = perm("lnrow", [1, 4 * 256], F32)
    qkraw = perm("qkraw", [128, 1024], F32)
    rms_sq = perm("rms_sq", [128, 512], F32)
    ssq = perm("ssq", [128, 8], F32)
    rstdqk = perm("rstdqk", [128, 8], F32)
    qn = perm("qn", [128, 512], BF16)
    kn = perm("kn", [128, 512], BF16)
    qr = perm("qr", [128, 512], F32)
    kr = perm("kr", [128, 512], F32)
    rot_t = perm("rot_t", [128, 256], F32)
    px = perm("px", [128, 2 * NT], F32)
    psq = perm("psq", [128, 2 * NT], F32)
    sqt = perm("sqt", [128, 2 * NT], F32)
    mcols = perm("mcols", [128, 8], F32)
    negw = perm("negw", [128, 1536], BF16)
    fc2b_sb = perm("fc2b_sb", [8, D], BF16)
    gate_sb = perm("gate_w_sb", [128, 16 * NE], F32)
    alpha_sb = perm("alpha_sb", [128, NE], F32)
    gtok = perm("gtok", [128, 8 * NE], F32)
    gprob = perm("gprob", [128, 8 * NE], F32)
    gred = perm("gred", [128, 8], F32)
    gmask = perm("gmask", [128, NE], F32)
    gtmp = perm("gtmp", [128, NE], F32)
    ctok = perm("ctok", [128, 8 * NE], F32)
    iota8c = perm("iota8c", [128, NE], F32)
    eCm1c = perm("eCm1c", [128, 64], F32)
    bndc = perm("bndc", [128, 64], F32)
    i128c = perm("i128c", [128, 1], F32)
    mtop = perm("mtop", [128, 64], F32)
    mtopb = perm("mtopb", [128, 64], BF16)
    gcum = perm("gcum", [128, 64], F32)

    m1b = perm("m1b", [128, 64], F32)
    m2b = perm("m2b", [128, 64], F32)
    grow = perm("grow", [1, 128], F32)
    mergec = perm("mergec", [128, 16], F32)
    gidx = perm("gidx", [128, 16], F32)
    gidxi = perm("gidxi", [128, 16], mybir.dt.int32)
    tkid = perm("tkid", [128, 8], F32)
    tki = perm("tki", [128, 24], F32)
    tkii = perm("tkii", [128, 24], mybir.dt.int32)

    fc1b_sb = perm("fc1b_sb", [128, NE * 16], F32)
    gelu_sc = perm("gelu_sc", [128, 512], BF16)
    rrow = perm("rrow", [1, 512], F32)
    rrowb = perm("rrowb", [1, 512], BF16)
    offr = at("offr", [128, 64], F32, _perm_off["gprob"])
    mrowb = at("mrowb", [1, 256], BF16, _perm_off["gelu_sc"])
    sgix = at("sgix", [128, 16], F32, _perm_off["mtopb"])
    sgixi = at("sgixi", [128, 16], mybir.dt.int32, _perm_off["mtopb"] + 64)
    wsumb = at("wsumb", [1, 1536], BF16, _perm_off["rrow"])
    lam_prod = at("lam_prod", [1, 256], F32, _perm_off["qkraw"])
    fdum = perm("fdum", [1, 4], F32)
    epsc = perm("epsc", [128, 4], F32)
    nc.const_aps.register(F32, 0.0, epsc[:, 2:3])
    nc.const_aps.register(F32, 1.0, epsc[:, 3:4])
    PERM_END = _cursor[0]
    assert PERM_END <= ABS0 + 45056, (PERM_END, ABS0)

    OF = ABS0 + 45056
    qF = at("qF", [128, 4 * T], BF16, OF)
    kF = at("kF", [128, 4 * T], BF16, OF + 32768)
    vTok = at("vTok", [128, 32 * 512], BF16, OF + 65536)
    wqkv = at("wqkv", [128, 16 * 1536], BF16, OF + 98304)
    xch = [at("xch0", [128, 16 * NT], BF16, OF + 147456),
           at("xch1", [128, 16 * NT], BF16, OF + 155648)]
    assert OF + 163840 <= nc.sbuf_top, (OF, nc.sbuf_top)
    yF = at("yF", [128, 4 * T], BF16, OF + 98304)
    etile = [at(f"etile{i}", [128, 512], BF16, OF + 131072 + i * 1024) for i in range(4)]
    rrep = at("rrep", [128, 512], F32, OF + 135168)
    t2sb = at("t2sb", [128, 2 * 512], F32, OF + 137216)
    usb = at("usb", [128, 512], F32, OF + 141312)
    esum = [at(f"esum{i}", [128, 512], F32, OF + 143360 + i * 2048) for i in range(2)]
    wosl = [at(f"wosl{i}", [128, D], BF16, OF + 131072 + i * 4096) for i in range(4)]
    pout8 = at("pout8", [128, 16 * 512], BF16, OF + 65536)
    xg_tm = at("xg_tm", [128, NSB * D], F32, OF)
    xge = [at(f"xge{i}", [128, 16 * CAP], BF16, OF + 24576 + i * 12288) for i in range(2)]
    y2sb = [at(f"y2sb{i}", [128, NSB * D], BF16, OF + 49152 + i * 12288) for i in range(2)]
    mgt = [at(f"mgt{i}", [128, D], BF16, OF + 73728 + i * 4096) for i in range(4)]
    hsb = [at(f"hsb{i}", [128, 16 * CAP], BF16, OF + 98304 + i * 12288) for i in range(2)]
    fc1sl = [at(f"fc1sl{i}", [128, 16 * 128], BF16, OF + 147456 + i * 4096) for i in range(4)]
    fc2sl = [at(f"fc2sl{i}", [128, 8 * 512], BF16, OF + 131072 + i * 8192) for i in range(2)]
    xtf = at("xtf", [128, 2 * NT], F32, _perm_off["qn"])
    xtm = [at(f"xtm{i}", [128, 512], F32, _perm_off["qr"] + i * 2048) for i in range(2)]
    xrtile = [at(f"xrtile{i}", [128, D], F32, OF + 98304 + i * 8192) for i in range(2)]
    rtile = [at(f"rtile{i}", [128, D], BF16, OF + 114688 + i * 4096) for i in range(2)]

    from contextlib import ExitStack
    stack = ExitStack()
    en = stack.enter_context
    with stack:
        PS = [en(nc.psum_tensor(f"PS{i}", [128, 512], F32)) for i in range(8)]
        block = en(nc.Block())

        bufs = {}

        def Bf(name):
            if name not in bufs:
                bufs[name] = Buf(name)
            return bufs[name]

        pbuf = [Bf(f"PS{i}") for i in range(8)]
        fd_b = Bf("fdum")
        em.do("sync", lambda e: e.dma_start(out=fdum[:], in_=c_fz.ap()), wr=[fd_b], dma=True, tag="fdum_init")
        em.fence_fn = lambda e: e.copy(fdum[0:1, 0:1], fdum[0:1, 1:2])
        em.fence_deps = (fd_b,)

        c_b = Bf("consts")
        em.do("sync", lambda e: e.dma_start(out=consts[:, 0:128], in_=c_tri.ap()), wr=[c_b], dma=True)
        em.do("sync", lambda e: e.dma_start(out=consts[:, 128:256], in_=c_id.ap()), wr=[c_b], dma=True)
        em.do("sync", lambda e: e.dma_start(out=consts[:, 256:257], in_=c_mean_c.ap()), wr=[c_b], dma=True)
        em.do("sync", lambda e: e.dma_start(out=consts[:, 257:258], in_=c_ones_c.ap()), wr=[c_b], dma=True)
        em.do("sync", lambda e: e.dma_start(out=consts[0:1, 258:386], in_=c_ones_r.ap()), wr=[c_b], dma=True)
        TRI = consts[:, 0:128]
        IDN = consts[:, 128:256]
        MEANC = consts[:, 256:257]
        ONESC = consts[:, 257:258]
        ONESR = consts[0:1, 258:386]
        cf_b = Bf("constsf")
        em.do("sync", lambda e: e.dma_start(out=constsf[0:1, 0:128], in_=c_ones_rf.ap()), wr=[cf_b], dma=True)
        em.do("sync", lambda e: e.dma_start(out=constsf[:, 128:256], in_=c_idf.ap()), wr=[cf_b], dma=True)
        ONESRF = constsf[0:1, 0:128]
        IDNF = constsf[:, 128:256]
        ind_b = Bf("ind_sb")
        em.do("sync", lambda e: e.dma_start(out=ind_sb[:], in_=c_ind.ap()), wr=[ind_b], dma=True)
        eps_b = Bf("epsc")
        em.do("sync", lambda e: e.dma_start(out=epsc[:], in_=c_eps.ap()), wr=[eps_b], dma=True)
        rt_c = Bf("routing_consts")
        em.do("sync", lambda e: e.dma_start(out=iota8c[:], in_=c_iota8.ap()), wr=[rt_c], dma=True)
        em.do("sync", lambda e: e.dma_start(out=eCm1c[:], in_=c_eCm1.ap()), wr=[rt_c], dma=True)
        em.do("sync", lambda e: e.dma_start(out=bndc[:], in_=c_bnd.ap()), wr=[rt_c], dma=True)
        em.do("sync", lambda e: e.dma_start(out=i128c[:], in_=c_i128.ap()), wr=[rt_c], dma=True)
        mt_b = Bf("mtop")
        em.do("vector", lambda e: e.memset(mtop[:], 0.0), wr=[mt_b])
        toki_b = Bf("tok_i")
        em.do("sync", lambda e: e.dma_start(
            out=d_toki.ap()[0:NE * CAP, :].rearrange("(p c) one -> p (c one)", p=128),
            in_=mtop[:, 0:NE * NSB]),
            rd=[mt_b], wr=[toki_b], dma=True, tag="toki_zero")

        rope_b = Bf("ropes")
        em.do("gpsimd", lambda e: e.dma_start(
            out=ropes[:], in_=d_rope.ap().rearrange("p a b c -> p (a b c)")), wr=[rope_b], dma=True)
        lqk_b = Bf("lnrow")
        em.do("sync", lambda e: e.dma_start(out=lnrow[:], in_=d_lqk.ap()), wr=[lqk_b], dma=True)
        gate_b, alpha_b = Bf("gate"), Bf("alpha")
        em.do("gpsimd", lambda e: e.dma_start(
            out=gate_sb[:], in_=d_gate.ap().rearrange("p a b -> p (a b)")), wr=[gate_b], dma=True)
        em.do("sync", lambda e: e.dma_start(out=alpha_sb[:], in_=d_alpha.ap()), wr=[alpha_b], dma=True)
        fc1b_b, fc2b_b = Bf("fc1b"), Bf("fc2b")
        em.do("sync", lambda e: e.dma_start(out=fc1b_sb[:], in_=d_fc1b.ap()), wr=[fc1b_b], dma=True)
        em.do("gpsimd", lambda e: e.dma_start(out=fc2b_sb[:], in_=d_fc2b.ap()), wr=[fc2b_b], dma=True)
        wqkv_b = Bf("wqkv")
        for q4 in range(4):
            em.do("gpsimd", lambda e, q4=q4: e.dma_start(
                out=wqkv[:].rearrange("p (jc w) -> p jc w", jc=16)[:, q4 * 4:(q4 + 1) * 4, :],
                in_=d_wqkv.ap()[:, q4 * 4:(q4 + 1) * 4, :]), wr=[wqkv_b], dma=True, tag="wqkv_ld")

        lam_b, lpr_b, nlam_b = Bf("lam"), Bf("lam_prod"), Bf("nlam")
        em.do("vector", lambda e: e.tensor_mul(lam_prod[0:1, 0:256], lnrow[0:1, 0:256], lnrow[0:1, 256:512]),
              rd=[lqk_b], wr=[lpr_b])
        em.do("vector", lambda e: e.reduce_sum(lam_sb[0:1, 0:1], lam_prod[0:1, 0:256], axis=AXX),
              rd=[lpr_b], wr=[lam_b])
        em.do("vector", lambda e: e.tensor_mul(lam_prod[0:1, 0:256], lnrow[0:1, 512:768], lnrow[0:1, 768:1024]),
              rd=[lqk_b, lam_b], wr=[lpr_b])
        em.do("vector", lambda e: e.reduce_sum(lam_sb[0:1, 1:2], lam_prod[0:1, 0:256], axis=AXX),
              rd=[lpr_b], wr=[lam_b])
        em.do("scalar", lambda e: e.activation(lam_sb[0:1, 2:4], lam_sb[0:1, 0:2], AF.Exp),
              rd=[lam_b, eps_b], wr=[lam_b])
        em.do("vector", lambda e: e.tensor_sub(lam_sb[0:1, 4:5], lam_sb[0:1, 3:4], lam_sb[0:1, 2:3]),
              rd=[lam_b], wr=[lam_b])
        em.do("vector", lambda e: e.tensor_scalar_add(lam_sb[0:1, 5:6], lam_sb[0:1, 4:5], -LAM0),
              rd=[lam_b], wr=[lam_b])
        em.do("tensor", lambda e: e.matmul(PS[7][:, 0:1], ONESRF, lam_sb[0:1, 5:6], start=True, stop=True),
              rd=[lam_b, cf_b], wr=[pbuf[7]])
        em.do("scalar", lambda e: e.copy(nlam_col[:], PS[7][:, 0:1]), rd=[pbuf[7], eps_b], wr=[nlam_b])

        ONESF = epsc[:, 3:4]
        wsb_b, negw_b = Bf("wsumb"), Bf("negw")
        for co3 in range(3):
            for jc in range(16):
                rhs = wqkv[:].rearrange("p (jc w) -> p jc w", jc=16)[:, jc, co3 * 512:(co3 + 1) * 512]
                em.do("tensor", lambda e, rhs=rhs, co3=co3, jc=jc: e.matmul(
                    PS[co3][0:1, 0:512], ONESC, rhs, start=(jc == 0), stop=(jc == 15)),
                    rd=[wqkv_b, c_b], wr=[pbuf[co3]], tag="wsum_mm")
            em.do("scalar", lambda e, co3=co3: e.activation(
                wsumb[0:1, co3 * 512:(co3 + 1) * 512], PS[co3][0:1, 0:512], AF.Copy, scale=-1.0),
                rd=[pbuf[co3], eps_b], wr=[wsb_b], tag="wsum_ev")
            em.do("tensor", lambda e, co3=co3: e.matmul(
                PS[3 + co3][:], ONESR, wsumb[0:1, co3 * 512:(co3 + 1) * 512], start=True, stop=True),
                rd=[wsb_b, c_b], wr=[pbuf[3 + co3]], tag="negw_mm")
            em.do("vector", lambda e, co3=co3: e.tensor_copy(
                negw[:, co3 * 512:(co3 + 1) * 512], PS[3 + co3][:]),
                rd=[pbuf[3 + co3]], wr=[negw_b], tag="negw_ev")

        lnrow_b = lqk_b
        px_b, psq_b, mcol_b = Bf("px"), Bf("psq"), Bf("mcols")
        sqt_b = [Bf("sqt0"), Bf("sqt1")]

        def stats_partials(xv, xin_b, cp, n, xt=None):
            pxv = px[:, cp * NT: cp * NT + n]
            psqv = psq[:, cp * NT: cp * NT + n]
            xf = (xt if xt is not None else xch[cp])[:].rearrange(
                "p (jc w) -> p w jc", jc=16)
            em.do("vector", lambda e, xf=xf, pxv=pxv: e.tensor_reduce(
                pxv, xf, axis=AXX, op=ALU.add), rd=[xin_b], wr=[px_b], tag="st_x")
            for jc in range(16):
                x = xv(jc)
                if jc == 0:
                    em.do("scalar", lambda e, x=x, psqv=psqv: e.activation(psqv, x, AF.Square),
                          rd=[xin_b, eps_b], wr=[psq_b], tag="st_sq0")
                else:
                    pj = jc % 2
                    sv = sqt[:, pj * NT:pj * NT + n]
                    em.do("scalar", lambda e, x=x, sv=sv: e.activation(sv, x, AF.Square),
                          rd=[xin_b, eps_b], wr=[sqt_b[pj]], tag="st_sq")
                    em.do("gpsimd", lambda e, psqv=psqv, sv=sv: e.tensor_add(psqv, psqv, sv),
                          rd=[sqt_b[pj], psq_b], wr=[psq_b], tag="st_sqa")

        def stats_finish(cp, n):
            pxv = px[:, cp * NT: cp * NT + n]
            psqv = psq[:, cp * NT: cp * NT + n]
            em.do("tensor", lambda e, pxv=pxv, n=n: e.matmul(
                PS[0][0:1, 0:n], ONESF, pxv, start=True, stop=True),
                rd=[px_b, eps_b], wr=[pbuf[0]], tag="st_mm_m")
            em.do("tensor", lambda e, psqv=psqv, n=n: e.matmul(
                PS[1][0:1, 0:n], ONESF, psqv, start=True, stop=True),
                rd=[psq_b, eps_b], wr=[pbuf[1]], tag="st_mm_sq")
            em.do("scalar", lambda e, n=n: e.activation(lnrow[0:1, 0:n], PS[0][0:1, 0:n],
                                                        AF.Copy, scale=1.0 / D),
                  rd=[pbuf[0], eps_b], wr=[lnrow_b], tag="row_m")
            em.do("scalar", lambda e, n=n: e.activation(lnrow[0:1, 256:256 + n], PS[1][0:1, 0:n],
                                                        AF.Copy, scale=1.0 / D),
                  rd=[pbuf[1], eps_b], wr=[lnrow_b], tag="row_msq")
            em.do("vector", lambda e, n=n: e.tensor_mul(lnrow[0:1, 768:768 + n], lnrow[0:1, 0:n],
                                                        lnrow[0:1, 0:n]), rd=[lnrow_b], wr=[lnrow_b])
            em.do("vector", lambda e, n=n: e.tensor_sub(lnrow[0:1, 256:256 + n], lnrow[0:1, 256:256 + n],
                                                        lnrow[0:1, 768:768 + n]), rd=[lnrow_b], wr=[lnrow_b])
            em.do("scalar", lambda e, n=n: e.activation(lnrow[0:1, 512:512 + n], lnrow[0:1, 256:256 + n],
                                                        AF.Ln, bias=epsc[0:1, 0:1]), rd=[lnrow_b, eps_b], wr=[lnrow_b])
            em.do("scalar", lambda e, n=n: e.activation(lnrow[0:1, 512:512 + n], lnrow[0:1, 512:512 + n],
                                                        AF.Exp, scale=-0.5), rd=[lnrow_b, eps_b], wr=[lnrow_b])
            em.do("scalar", lambda e, n=n: e.activation(mrowb[0:1, 0:n], lnrow[0:1, 0:n], AF.Copy),
                  rd=[lnrow_b, eps_b], wr=[Bf("mrowb")], tag="mrow_cast")

        def cols_tb(n_off, p):
            for ri, src in ((1, 512),):
                em.do("tensor", lambda e, ri=ri, src=src, n_off=n_off: e.matmul(
                    PS[2 + ri][:, 0:1], lnrow[0:1, src + n_off: src + n_off + 128],
                    epsc[0:1, 3:4], start=True, stop=True),
                    rd=[lnrow_b, eps_b], wr=[pbuf[2 + ri]], tag="col_tr")
                em.do("scalar", lambda e, ri=ri, p=p: e.copy(
                    mcols[:, p * 4 + ri: p * 4 + ri + 1], PS[2 + ri][:, 0:1]),
                    rd=[pbuf[2 + ri], eps_b], wr=[mcol_b], tag="col_ev")

        def xv16(t, w):
            return t[:].rearrange("p (jc w) -> p jc w", jc=16, w=w)

        em.phase = "p12"
        xch_b = [Bf("xch0"), Bf("xch1")]
        qF_b, kF_b, vT_b, yF_b = Bf("qF"), Bf("kF"), Bf("vTok"), Bf("yF")
        qkraw_b, rsq_b, ssq_b, rstd_b = Bf("qkraw"), Bf("rms_sq"), Bf("ssq"), Bf("rstdqk")
        qn_b, kn_b, qr_b, kr_b, rot_b = Bf("qn"), Bf("kn"), Bf("qr"), Bf("kr"), Bf("rot_t")

        COS = ropes[:].rearrange("p (s b c) -> p s b c", s=2, b=4)[:, 0]
        SIN = ropes[:].rearrange("p (s b c) -> p s b c", s=2, b=4)[:, 1]

        pend_tr = [None]

        def emit_tr(tb):
            for (src, sb_, dstF, dF_b) in ((qr, qr_b, qF, qF_b), (kr, kr_b, kF, kF_b)):
                for chb in range(4):
                    pt = PS[7][:, chb * 128:(chb + 1) * 128]
                    em.do("tensor", lambda e, src=src, chb=chb, pt=pt: e.transpose(
                        pt, src[:, chb * 128:(chb + 1) * 128], IDNF),
                        rd=[sb_, cf_b], wr=[pbuf[7]], tag="qk_tr")
                ov = dstF[:].rearrange("p (c t) -> p c t", c=4)[:, :, tb * 128:(tb + 1) * 128]
                em.do("vector", lambda e, ov=ov: e.tensor_copy(ov, PS[7][:].rearrange(
                    "p (c t) -> p c t", c=4)), rd=[pbuf[7]], wr=[dF_b], tag="qk_tr_ev")

        for tc in range(T // NT):
            cp = tc % 2
            xb = xch_b[cp]
            for q4 in range(4):
                em.do("gpsimd", lambda e, tc=tc, q4=q4, cp=cp: e.dma_start(
                    out=xv16(xch[cp], NT)[:, q4 * 4:(q4 + 1) * 4, :],
                    in_=d_xT.ap()[:, q4 * 4:(q4 + 1) * 4, tc * NT:(tc + 1) * NT]),
                    wr=[xb], dma=True, tag="x_load")
            stats_partials(lambda jc, cp=cp: xch[cp][:, jc * NT:(jc + 1) * NT], xb, cp, NT)

            for tbl in range(NT // 128):
                tb = tc * (NT // 128) + tbl
                p = tbl
                for jc in range(16):
                    lhs = xv16(xch[cp], NT)[:, jc, tbl * 128:(tbl + 1) * 128]
                    for (ps_i, co) in ((4, 0), (5, 512), (6, 1024)):
                        rhs = xv16(wqkv, 1536)[:, jc, co:co + 512]
                        em.do("tensor", lambda e, lhs=lhs, rhs=rhs, ps_i=ps_i, jc=jc: e.matmul(
                            PS[ps_i][:], lhs, rhs, start=(jc == 0), stop=False),
                            rd=[xb, wqkv_b], wr=[pbuf[ps_i]], tag="proj_mm")
                if tbl == 0:
                    stats_finish(cp, NT)
                if pend_tr[0] is not None:
                    emit_tr(pend_tr[0])
                    pend_tr[0] = None
                cols_tb(tbl * 128, p)
                for (ps_i, co) in ((4, 0), (5, 512), (6, 1024)):
                    em.do("tensor", lambda e, ps_i=ps_i, co=co, tbl=tbl: e.matmul(
                        PS[ps_i][:], mrowb[0:1, tbl * 128:(tbl + 1) * 128],
                        wsumb[0:1, co:co + 512], start=False, stop=True),
                        rd=[Bf("mrowb"), wsb_b], wr=[pbuf[ps_i]], tag="mcorr_mm")
                rc = mcols[:, p * 4 + 1: p * 4 + 2]
                em.do("scalar", lambda e: e.copy(qkraw[:, 0:512], PS[4][:]),
                      rd=[pbuf[4], eps_b], wr=[qkraw_b], tag="qk_ev")
                em.do("scalar", lambda e: e.copy(qkraw[:, 512:1024], PS[5][:]),
                      rd=[pbuf[5], eps_b], wr=[qkraw_b], tag="qk_ev")
                vv = vTok[:, tb * 512:(tb + 1) * 512]
                em.do("vector", lambda e, vv=vv, rc=rc: e.tensor_scalar_mul(vv, PS[6][:], rc),
                      rd=[pbuf[6], mcol_b], wr=[vT_b], tag="v_scale")
                em.do("vector", lambda e: e.tensor_mul(rms_sq[:], qkraw[:, 0:512], qkraw[:, 0:512]),
                      rd=[qkraw_b], wr=[rsq_b], tag="rms_sq_q")
                em.do("vector", lambda e: e.tensor_reduce(
                    ssq[:, 0:4], rms_sq[:].rearrange("p (b c) -> p b c", b=4), axis=AXX, op=ALU.add),
                    rd=[rsq_b], wr=[ssq_b], tag="rms_red_q")
                em.do("vector", lambda e: e.tensor_mul(rms_sq[:], qkraw[:, 512:1024], qkraw[:, 512:1024]),
                      rd=[qkraw_b], wr=[rsq_b], tag="rms_sq_k")
                em.do("vector", lambda e: e.tensor_reduce(
                    ssq[:, 4:8], rms_sq[:].rearrange("p (b c) -> p b c", b=4), axis=AXX, op=ALU.add),
                    rd=[rsq_b], wr=[ssq_b], tag="rms_red_k")
                em.do("scalar", lambda e: e.activation(rstdqk[:], ssq[:], AF.Ln, scale=1.0 / HH, bias=epsc[:, 1:2]),
                      rd=[ssq_b, eps_b], wr=[rstd_b])
                em.do("scalar", lambda e: e.activation(rstdqk[:], rstdqk[:], AF.Exp, scale=-0.5),
                      rd=[rstd_b, eps_b], wr=[rstd_b])
                for blk in range(4):
                    em.do("vector", lambda e, blk=blk: e.tensor_scalar_mul(
                        qn[:, blk * 128:(blk + 1) * 128], qkraw[:, blk * 128:(blk + 1) * 128],
                        rstdqk[:, blk:blk + 1]), rd=[qkraw_b, rstd_b], wr=[qn_b], tag="rms_ap_q")
                    em.do("vector", lambda e, blk=blk: e.tensor_scalar_mul(
                        kn[:, blk * 128:(blk + 1) * 128], qkraw[:, 512 + blk * 128:512 + (blk + 1) * 128],
                        rstdqk[:, 4 + blk:5 + blk]), rd=[qkraw_b, rstd_b], wr=[kn_b], tag="rms_ap_k")
                for (src, sb_, dst, db_) in ((qn, qn_b, qr, qr_b), (kn, kn_b, kr, kr_b)):
                    s4 = src[:].rearrange("p (b h c) -> p b h c", b=4, h=2)
                    d4 = dst[:].rearrange("p (b h c) -> p b h c", b=4, h=2)
                    x1, x2 = s4[:, :, 0], s4[:, :, 1]
                    o1, o2 = d4[:, :, 0], d4[:, :, 1]
                    rt = rot_t[:].rearrange("p (b c) -> p b c", b=4)
                    em.do("vector", lambda e, x1=x1, o1=o1: e.tensor_mul(o1, x1, COS),
                          rd=[sb_, rope_b], wr=[db_], tag="rot")
                    em.do("vector", lambda e, x2=x2, rt=rt: e.tensor_mul(rt, x2, SIN),
                          rd=[sb_, rope_b], wr=[rot_b], tag="rot")
                    em.do("vector", lambda e, o1=o1, rt=rt: e.tensor_add(o1, o1, rt),
                          rd=[rot_b, db_], wr=[db_], tag="rot")
                    em.do("vector", lambda e, x2=x2, o2=o2: e.tensor_mul(o2, x2, COS),
                          rd=[sb_, rope_b], wr=[db_], tag="rot")
                    em.do("vector", lambda e, x1=x1, rt=rt: e.tensor_mul(rt, x1, SIN),
                          rd=[sb_, rope_b], wr=[rot_b], tag="rot")
                    em.do("vector", lambda e, o2=o2, rt=rt: e.tensor_sub(o2, o2, rt),
                          rd=[rot_b, db_], wr=[db_], tag="rot")
                pend_tr[0] = tb
        if pend_tr[0] is not None:
            emit_tr(pend_tr[0])
            pend_tr[0] = None

        em.phase = "p12b"
        gt_b = Bf("gtok")
        xtmd_b = Bf("xn_tm")
        mr_b = Bf("mrrep")
        xtf_b = [Bf("xtf0"), Bf("xtf1")]
        xtm_b = [Bf("xtm0"), Bf("xtm1")]
        em.overlay([mr_b], [rsq_b], tag="ovl_mr")
        em.overlay(xtf_b, [qn_b, kn_b], tag="ovl_xtf")
        em.overlay(xtm_b, [qr_b, kr_b], tag="ovl_xtm")
        mrrep = rms_sq

        def emit_slice_chunk(k):
            em.phase = "p12b"
            cp = k % 2
            xb = xch_b[cp]
            for q4 in range(4):
                em.do("gpsimd", lambda e, k=k, q4=q4, cp=cp: e.dma_start(
                    out=xv16(xch[cp], NT)[:, q4 * 4:(q4 + 1) * 4, :],
                    in_=d_xresT.ap()[:, q4 * 4:(q4 + 1) * 4, k * NT:(k + 1) * NT]),
                    wr=[xb], dma=True, tag="xsl_load")
            stats_partials(lambda jc, cp=cp: xch[cp][:, jc * NT:(jc + 1) * NT], xb, cp, NT)
            stats_finish(cp, NT)
            for ri, src in ((0, 0), (1, 512)):
                em.do("tensor", lambda e, ri=ri, src=src: e.matmul(
                    PS[2 + ri][:, 0:NT], ONESRF, lnrow[0:1, src:src + NT], start=True, stop=True),
                    rd=[lnrow_b, cf_b], wr=[pbuf[2 + ri]], tag="mr_rep")
                em.do("scalar", lambda e, ri=ri: e.copy(mrrep[:, ri * NT:(ri + 1) * NT], PS[2 + ri][:, 0:NT]),
                      rd=[pbuf[2 + ri], eps_b], wr=[mr_b], tag="mr_ev")
            for jc in range(16):
                pj = jc % 2
                xt = xtf[:, pj * NT:(pj + 1) * NT]
                x = xch[cp][:, jc * NT:(jc + 1) * NT]
                em.do("vector", lambda e, xt=xt, x=x: e.tensor_sub(xt, x, mrrep[:, 0:NT]),
                      rd=[xb, mr_b], wr=[xtf_b[pj]], tag="xn_sub")
                em.do("gpsimd", lambda e, xt=xt: e.tensor_mul(xt, xt, mrrep[:, NT:2 * NT]),
                      rd=[mr_b, xtf_b[pj]], wr=[xtf_b[pj]], tag="xn_mul")
                for tbl in range(2):
                    lhs = xtf[:, pj * NT + tbl * 128: pj * NT + (tbl + 1) * 128]
                    em.do("tensor", lambda e, lhs=lhs, jc=jc, tbl=tbl: e.matmul(
                        PS[4 + tbl][:, 0:NE], lhs, gate_sb[:, jc * NE:(jc + 1) * NE],
                        start=(jc == 0), stop=(jc == 15)),
                        rd=[xtf_b[pj], gate_b], wr=[pbuf[4 + tbl]], tag="gate_mm")
                    pt = PS[7 - tbl][:, (jc % 4) * 128:(jc % 4 + 1) * 128]
                    em.do("tensor", lambda e, lhs=lhs, pt=pt: e.transpose(pt, lhs, IDNF),
                          rd=[xtf_b[pj], cf_b], wr=[pbuf[7 - tbl]], tag="xn_tr")
                if jc % 4 == 3:
                    for tbl in range(2):
                        em.do("vector", lambda e, tbl=tbl: e.tensor_copy(xtm[tbl][:], PS[7 - tbl][:]),
                              rd=[pbuf[7 - tbl]], wr=[xtm_b[tbl]], tag="xn_tr_ev")
                        tok0 = (k * 2 + tbl) * 128
                        c0 = (jc - 3) * 128
                        em.do("sync", lambda e, tbl=tbl, tok0=tok0, c0=c0: e.dma_start(
                            out=d_xtm.ap()[tok0:tok0 + 128, c0:c0 + 512], in_=xtm[tbl][:]),
                            rd=[xtm_b[tbl]], wr=[xtmd_b], dma=True, tag="xtm_store")
            for tbl in range(2):
                tb = k * 2 + tbl
                em.do("scalar", lambda e, tb=tb, tbl=tbl: e.copy(
                    gtok[:, tb * NE:(tb + 1) * NE], PS[4 + tbl][:, 0:NE]),
                    rd=[pbuf[4 + tbl], eps_b], wr=[gt_b], tag="gate_ev")
            emit_gate_tb(k * 2)
            emit_gate_tb(k * 2 + 1)
            em.phase = "p3"

        gp_b, gr_b, gm_b, gtmp_b = Bf("gprob"), Bf("gred"), Bf("gmask"), Bf("gtmp")
        m1_b, m2_b = Bf("m1b"), Bf("m2b")

        def emit_gate_tb(tb):
            em.phase = "p12c"
            g = gtok[:, tb * NE:(tb + 1) * NE]
            p = gprob[:, tb * NE:(tb + 1) * NE]
            c = ctok[:, tb * NE:(tb + 1) * NE]
            m1v = m1b[:, tb * NE:(tb + 1) * NE]
            m2v = m2b[:, tb * NE:(tb + 1) * NE]
            mtv = mtop[:, tb * NE:(tb + 1) * NE]
            em.do("vector", lambda e, g=g: e.reduce_max(gred[:, 0:1], g, axis=AXX), rd=[gt_b], wr=[gr_b])
            em.do("vector", lambda e, g=g, p=p: e.tensor_scalar(p, g, gred[:, 0:1], None, ALU.subtract),
                  rd=[gt_b, gr_b], wr=[gp_b])
            em.do("scalar", lambda e, p=p: e.activation(p, p, AF.Exp), rd=[gp_b, eps_b], wr=[gp_b])
            em.do("vector", lambda e, p=p: e.reduce_sum(gred[:, 1:2], p, axis=AXX), rd=[gp_b], wr=[gr_b])
            em.do("vector", lambda e: e.reciprocal(gred[:, 2:3], gred[:, 1:2]), rd=[gr_b], wr=[gr_b])
            em.do("vector", lambda e, p=p: e.tensor_scalar(p, p, gred[:, 2:3], None, ALU.mult),
                  rd=[gp_b, gr_b], wr=[gp_b])
            em.do("vector", lambda e, p=p: e.reduce_max(gred[:, 3:4], p, axis=AXX), rd=[gp_b], wr=[gr_b])
            em.do("vector", lambda e, p=p, m1v=m1v: e.tensor_scalar(m1v, p, gred[:, 3:4], None, ALU.is_ge),
                  rd=[gp_b, gr_b], wr=[m1_b], tag="m1")
            em.do("vector", lambda e, p=p: e.tensor_scalar(gmask[:], p, gred[:, 3:4], None, ALU.is_lt),
                  rd=[gp_b, gr_b], wr=[gm_b])
            em.do("vector", lambda e, p=p: e.tensor_mul(gtmp[:], p, gmask[:]), rd=[gp_b, gm_b], wr=[gtmp_b])
            em.do("vector", lambda e: e.reduce_max(gred[:, 4:5], gtmp[:], axis=AXX), rd=[gtmp_b], wr=[gr_b])
            em.do("vector", lambda e, p=p, mtv=mtv: e.tensor_scalar(mtv, p, gred[:, 4:5], None, ALU.is_ge),
                  rd=[gp_b, gr_b], wr=[mt_b], tag="mtop")
            em.do("vector", lambda e, p=p, mtv=mtv, m1v=m1v, m2v=m2v: e.tensor_sub(m2v, mtv, m1v),
                  rd=[mt_b, m1_b], wr=[m2_b], tag="m2")
            em.do("vector", lambda e, p=p, mtv=mtv: e.tensor_mul(gtmp[:], p, mtv),
                  rd=[gp_b, mt_b], wr=[gtmp_b])
            em.do("vector", lambda e: e.reduce_sum(gred[:, 5:6], gtmp[:], axis=AXX), rd=[gtmp_b], wr=[gr_b])
            em.do("vector", lambda e: e.reciprocal(gred[:, 6:7], gred[:, 5:6]), rd=[gr_b], wr=[gr_b])
            em.do("vector", lambda e: e.tensor_scalar(gtmp[:], gtmp[:], gred[:, 6:7], None, ALU.mult),
                  rd=[gtmp_b, gr_b], wr=[gtmp_b])
            em.do("vector", lambda e, c=c: e.tensor_mul(c, gtmp[:], alpha_sb[:]),
                  rd=[gtmp_b, alpha_b], wr=[ct_b], tag="ctok")


            em.phase = "p3"

        def emit_routing():
            em.phase = "p12c"
            mtb_b, gc_b, off_b, grow_b = Bf("mtopb"), Bf("gcum"), Bf("offr"), Bf("grow")
            mer_b, gx_b, gxi_b, tkd_b = Bf("mergec"), Bf("gidx"), Bf("gidxi"), Bf("tkii")
            tki_b = Bf("tki")
            em.do("vector", lambda e: e.tensor_copy(mtopb[:], mtop[:]), rd=[mt_b], wr=[mtb_b])
            em.do("tensor", lambda e: e.matmul(PS[6][:, 0:64], TRI, mtopb[:], start=True, stop=True),
                  rd=[mtb_b, c_b], wr=[pbuf[6]], tag="cum_mm")
            em.do("vector", lambda e: e.tensor_copy(gcum[:], PS[6][:, 0:64]), rd=[pbuf[6]], wr=[gc_b])
            em.do("tensor", lambda e: e.matmul(PS[7][0:1, 0:64], ONESC, mtopb[:], start=True, stop=True),
                  rd=[mtb_b, c_b], wr=[pbuf[7]], tag="tot_mm")
            em.do("scalar", lambda e: e.copy(grow[0:1, 0:64], PS[7][0:1, 0:64]), rd=[pbuf[7], eps_b], wr=[grow_b])
            em.do("vector", lambda e: e.memset(grow[0:1, 64:72], 0.0), rd=[grow_b], wr=[grow_b])
            for tb in range(1, 8):
                em.do("vector", lambda e, tb=tb: e.tensor_add(
                    grow[0:1, 64 + tb * 8:72 + tb * 8], grow[0:1, 56 + tb * 8:64 + tb * 8],
                    grow[0:1, (tb - 1) * 8:tb * 8]), rd=[grow_b], wr=[grow_b], tag="off_acc")
            em.do("tensor", lambda e: e.matmul(PS[5][:, 0:64], ONESRF, grow[0:1, 64:128], start=True, stop=True),
                  rd=[grow_b, cf_b], wr=[pbuf[5]], tag="off_mm")
            em.do("vector", lambda e: e.tensor_copy(offr[:], PS[5][:, 0:64]), rd=[pbuf[5]], wr=[off_b])
            em.do("vector", lambda e: e.tensor_add(gcum[:], gcum[:], offr[:]), rd=[off_b, gc_b], wr=[gc_b])
            em.do("vector", lambda e: e.tensor_add(gcum[:], gcum[:], eCm1c[:]), rd=[rt_c, gc_b], wr=[gc_b])
            em.do("vector", lambda e: e.tensor_tensor(offr[:], gcum[:], bndc[:], ALU.is_ge),
                  rd=[rt_c, gc_b], wr=[off_b], tag="ovf")
            em.do("vector", lambda e: e.scalar_tensor_tensor(gcum[:], offr[:], 524288.0, gcum[:],
                                                             ALU.mult, ALU.add),
                  rd=[off_b, gc_b], wr=[gc_b], tag="ovf_push")
            for tb in range(8):
                gvs = gcum[:, tb * NE:(tb + 1) * NE]
                cs = ctok[:, tb * NE:(tb + 1) * NE]
                for k_, mb_, mv in ((0, m1_b, m1b), (1, m2_b, m2b)):
                    mvs = mv[:, tb * NE:(tb + 1) * NE]
                    em.do("vector", lambda e, mvs=mvs, gvs=gvs: e.tensor_mul(gmask[:], mvs, gvs),
                          rd=[mb_, gc_b], wr=[gm_b])
                    em.do("vector", lambda e, tb=tb, k_=k_: e.reduce_sum(
                        gidx[:, tb * 2 + k_:tb * 2 + k_ + 1], gmask[:], axis=AXX), rd=[gm_b], wr=[gx_b])
                    em.do("vector", lambda e, mvs=mvs, cs=cs: e.tensor_mul(gmask[:], mvs, cs),
                          rd=[mb_, ct_b], wr=[gm_b])
                    em.do("vector", lambda e, tb=tb, k_=k_: e.reduce_sum(
                        mergec[:, tb * 2 + k_:tb * 2 + k_ + 1], gmask[:], axis=AXX), rd=[gm_b], wr=[mer_b])
            em.do("vector", lambda e: e.tensor_scalar(
                offr[:, 32:48], gidx[:], float(NE * CAP - 1), None, ALU.min),
                rd=[gx_b], wr=[off_b], tag="gidx_clamp")
            em.do("vector", lambda e: e.tensor_copy(gidxi[:], offr[:, 32:48]), rd=[off_b], wr=[gxi_b], tag="gidx_cast")
            sg_b, sgi_b = Bf("sgix"), Bf("sgixi")
            em.do("vector", lambda e: e.tensor_scalar(offr[:, 0:16], gidx[:], 1.0 / 128.0, -0.499,
                                                      ALU.mult, ALU.add),
                  rd=[gx_b], wr=[off_b], tag="sg0")
            em.do("vector", lambda e: e.tensor_copy(sgixi[:], offr[:, 0:16]), rd=[off_b], wr=[sgi_b], tag="sg1")
            em.do("vector", lambda e: e.tensor_copy(sgix[:], sgixi[:]), rd=[sgi_b], wr=[sg_b], tag="sg2")
            em.do("vector", lambda e: e.scalar_tensor_tensor(
                offr[:, 16:32], sgix[:], -128.0, gidx[:], ALU.mult, ALU.add),
                rd=[sg_b, gx_b], wr=[off_b], tag="sg3")
            em.do("vector", lambda e: e.scalar_tensor_tensor(
                sgix[:], offr[:, 16:32], float(NE * NSB) + 0.0, sgix[:], ALU.mult, ALU.add),
                rd=[off_b, sg_b], wr=[sg_b], tag="sg4")
            em.do("vector", lambda e: e.tensor_copy(sgixi[:], sgix[:]), rd=[sg_b], wr=[sgi_b], tag="sg5")
            for tb in range(8):
                em.do("vector", lambda e, tb=tb: e.tensor_scalar_add(
                    tkid[:, tb:tb + 1], i128c[:, 0:1], float(128 * tb)), rd=[rt_c], wr=[Bf("tkid")], tag="tkid")
            tkd_bb = Bf("tkid")
            for tb in range(8):
                for k_ in range(2):
                    em.do("gpsimd", lambda e, tb=tb, k_=k_: e.indirect_dma_start(
                        out=d_toki.ap(),
                        out_offset=bass.IndirectOffsetOnAxis(ap=sgixi[:, tb * 2 + k_:tb * 2 + k_ + 1], axis=0),
                        in_=tkid[:, tb:tb + 1], in_offset=None, oob_is_err=False),
                        rd=[sgi_b, tkd_bb], wr=[toki_b], dma=True, tag="toki_scat")
            em.do("sync", lambda e: e.dma_start(
                out=tki[:], in_=d_toki.ap()[0:NE * CAP, :].rearrange("(p c) one -> p (c one)", p=128)),
                rd=[toki_b], wr=[tki_b], dma=True, tag="tki_ld")
            em.do("vector", lambda e: e.tensor_copy(tkii[:], tki[:]), rd=[tki_b], wr=[tkd_b], tag="tki_cast")
            for nm, tt, bb in (("tki", tki, tki_b), ("gidx", gidx, gx_b), ("mergec", mergec, mer_b),
                               ("mtop", mtop, mt_b), ("gcum", gcum, gc_b), ("gtok", gtok, gt_b),
                               ("ctok", ctok, ct_b), ("grow", grow, grow_b)):
                t_ = dbg(nm, list(tt.shape), F32)
                if t_ is not None:
                    em.do("sync", lambda e, t_=t_, tt=tt: e.dma_start(out=t_.ap(), in_=tt[:]),
                          rd=[bb], wr=[Bf("dbgout")], dma=True)

            em.phase = "p3"

        fc1sl_b = [Bf(f"fc1sl{i}") for i in range(4)]
        fc2sl_b = [Bf(f"fc2sl{i}") for i in range(2)]

        def emit_prefetch():
            em.overlay(fc1sl_b, [xch_b[0], xch_b[1]], tag="ovl_fc1pre")
            for ib in range(4):
                em.do("gpsimd", lambda e, ib=ib: e.dma_start(
                    out=fc1sl[ib][:], in_=d_fc1.ap()[0, ib].rearrange("p jc i -> p (jc i)")),
                    wr=[fc1sl_b[ib]], dma=True, tag="fc1_load")

        for nm, tt, bb in (("qF", qF, qF_b), ("kF", kF, kF_b), ("vTok", vTok, vT_b)):
            t_ = dbg(nm, [128, 16384], BF16)
            if t_ is not None:
                em.do("sync", lambda e, t_=t_, tt=tt: e.dma_start(out=t_.ap(), in_=tt[:]), rd=[bb], wr=[Bf("dbgout")], dma=True)

        tkd_b, gxi_b, mer_b = Bf("tkii"), Bf("gidxi"), Bf("mergec")
        ct_b = Bf("ctok")

        em.phase = "p3"
        e_b = [Bf(f"etile{i}") for i in range(4)]
        es_b = [Bf(f"esum{i}") for i in range(2)]
        rrow_b, rrowb_b, rrep_b = Bf("rrow"), Bf("rrowb"), Bf("rrep")
        t2_b, u_b = Bf("t2sb"), Bf("usb")
        em.overlay([yF_b, rrep_b, t2_b, u_b] + e_b + es_b, [wqkv_b], tag="ovl_p3")
        em.overlay([rrow_b, rrowb_b], [wsb_b], tag="ovl_rrow")

        ecnt = [0]
        SCB = (2, 3, 5)
        ONESF = epsc[:, 3:4]

        def attn_scores(h, var, qc, pa0, pa1):
            chb = h * 2 + var
            nkb = 4 * qc + 4
            es, esb = esum[var], es_b[var]
            tiles = []

            def emit_pv(kb):
                et, etb, N, start_l = tiles[kb]
                for half in range(2):
                    pa = pa0 if half == 0 else pa1
                    vl = vTok[:, kb * 512 + h * 256 + half * 128: kb * 512 + h * 256 + (half + 1) * 128]
                    em.do("tensor", lambda e, vl=vl, et=et, pa=pa, N=N, start_l=start_l, kb=kb, nkb=nkb:
                          e.matmul(PS[pa][:, start_l:512], vl, et[:, 0:N],
                                   start=(kb == 0), stop=(kb == nkb - 1)),
                          rd=[vT_b, etb], wr=[pbuf[pa]], tag="pv_mm")

            for kb in range(nkb):
                start_l = max(0, 128 * kb - 512 * qc)
                N = 512 - start_l
                ps_s = SCB[kb % 3]
                ei = ecnt[0] % 4
                ecnt[0] += 1
                et, etb = etile[ei], e_b[ei]
                tiles.append((et, etb, N, start_l))
                lhs = kF[:, chb * T + kb * 128: chb * T + (kb + 1) * 128]
                rhs = qF[:, chb * T + qc * 512 + start_l: chb * T + (qc + 1) * 512]
                em.do("tensor", lambda e, lhs=lhs, rhs=rhs, ps_s=ps_s, N=N: e.matmul(
                    PS[ps_s][:, 0:N], lhs, rhs, start=True, stop=True),
                    rd=[kF_b, qF_b], wr=[pbuf[ps_s]], tag="score_mm")
                em.do("scalar", lambda e, ps_s=ps_s, et=et, N=N: e.activation(
                    et[:, 0:N], PS[ps_s][:, 0:N], AF.Exp, scale=SCALE),
                    rd=[pbuf[ps_s], eps_b], wr=[etb], tag="exp")
                if kb >= 4 * qc:
                    em.do("gpsimd", lambda e, et=et: e.tensor_mul(et[:, 0:128], et[:, 0:128], TRI),
                          rd=[etb, c_b], wr=[etb], tag="diag_mask")
                if kb == 0:
                    em.do("vector", lambda e, et=et, es=es: e.tensor_copy(es[:], et[:]),
                          rd=[etb], wr=[esb], tag="colsum0")
                else:
                    em.do("vector", lambda e, et=et, es=es, N=N, start_l=start_l:
                          e.tensor_add(es[:, start_l:512], es[:, start_l:512], et[:, 0:N]),
                          rd=[etb, esb], wr=[esb], tag="colsum")
                if kb >= 2:
                    emit_pv(kb - 2)
            emit_pv(nkb - 2)
            emit_pv(nkb - 1)

            def finish():
                em.do("tensor", lambda e, es=es: e.matmul(
                    PS[4][0:1, 0:512], ONESF, es[:], start=True, stop=True),
                    rd=[esb, eps_b], wr=[pbuf[4]], tag="colred_mm")
                em.do("vector", lambda e: e.reciprocal(rrow[:], PS[4][0:1, 0:512]),
                      rd=[pbuf[4]], wr=[rrow_b], tag="recip")
                em.do("vector", lambda e: e.tensor_copy(rrowb[:], rrow[:]), rd=[rrow_b], wr=[rrowb_b])
                em.do("tensor", lambda e: e.matmul(PS[4][:], ONESR, rrowb[:], start=True, stop=True),
                      rd=[rrowb_b, c_b], wr=[pbuf[4]], tag="rrep_mm")
                em.do("scalar", lambda e: e.copy(rrep[:], PS[4][:]), rd=[pbuf[4], eps_b], wr=[rrep_b], tag="rrep_ev")
            return finish

        for h in range(2):
            for qc in (range(8) if h == 0 else range(7, -1, -1)):
                gi = h * 8 + qc if h == 0 else 8 + (7 - qc)
                fin1 = attn_scores(h, 1, qc, 0, 1)
                fin0 = attn_scores(h, 0, qc, 6, 7)
                fin1()
                for half in range(2):
                    em.do("vector", lambda e, half=half: e.tensor_mul(
                        t2sb[:, half * 512:(half + 1) * 512], PS[half][:], rrep[:]),
                        rd=[pbuf[half], rrep_b], wr=[t2_b], tag="t2")
                fin0()
                for half in range(2):
                    pa = 6 if half == 0 else 7
                    em.do("vector", lambda e, pa=pa, half=half: e.tensor_mul(usb[:], PS[pa][:], rrep[:]),
                          rd=[pbuf[pa], rrep_b], wr=[u_b], tag="u")
                    yv = yF[:, (h * 2 + half) * T + qc * 512:(h * 2 + half) * T + (qc + 1) * 512]
                    em.do("vector", lambda e, yv=yv, half=half: e.scalar_tensor_tensor(
                        yv, t2sb[:, half * 512:(half + 1) * 512], nlam_col[:, 0:1], usb[:],
                        ALU.mult, ALU.add), rd=[t2_b, u_b, nlam_b], wr=[yF_b], tag="ycomb")
                if gi in (1, 3, 5, 7):
                    emit_slice_chunk((gi - 1) // 2)
                elif gi == 8:
                    emit_routing()
                elif gi == 9:
                    emit_prefetch()

        t_ = dbg("yF", [128, 4 * T], BF16)
        if t_ is not None:
            em.do("sync", lambda e, t_=t_: e.dma_start(out=t_.ap(), in_=yF[:]), rd=[yF_b], wr=[Bf("dbgout")], dma=True)

        em.phase = "p4"
        wo_b = [Bf(f"wosl{i}") for i in range(4)]
        po_b = [Bf(f"pout8_{i}") for i in range(16)]
        xg_b = Bf("xg_tm")
        xge_b = [Bf("xge0"), Bf("xge1")]
        em.overlay(wo_b, [rrep_b, t2_b, u_b] + e_b + es_b, tag="ovl_p4")
        em.overlay(po_b, [vT_b], tag="ovl_pout")
        em.overlay([xg_b], [qF_b], tag="ovl_xg")
        em.overlay(xge_b, [qF_b, kF_b], tag="ovl_xge")
        pfq_b = [Bf(f"p_full{q}") for q in range(4)]

        def emit_head(e_, part):
            par = e_ % 2
            if part == 0:
                for sblk in range(NSB):
                    em.do("gpsimd", lambda e, e_=e_, sblk=sblk: e.indirect_dma_start(
                        out=xg_tm[:, sblk * D:(sblk + 1) * D], out_offset=None,
                        in_=d_xtm.ap(),
                        in_offset=bass.IndirectOffsetOnAxis(
                            ap=tkii[:, e_ * NSB + sblk:e_ * NSB + sblk + 1], axis=0),
                        oob_is_err=False),
                        rd=[tkd_b, xtmd_b], wr=[xg_b], dma=True, tag="xg_gather")
                return
            for jc in range((part - 1) * 8, part * 8):
                pp = 6 + (jc % 2)
                for sblk in range(NSB):
                    src = xg_tm[:, sblk * D + jc * 128: sblk * D + (jc + 1) * 128]
                    em.do("tensor", lambda e, src=src, pp=pp, sblk=sblk: e.transpose(
                        PS[pp][:, sblk * 128:(sblk + 1) * 128], src, IDNF),
                        rd=[xg_b, cf_b], wr=[pbuf[pp]], tag="xg_tr")
                em.do("vector", lambda e, pp=pp, jc=jc, par=par: e.tensor_copy(
                    xge[par][:, jc * CAP:(jc + 1) * CAP], PS[pp][:, 0:CAP]),
                    rd=[pbuf[pp]], wr=[xge_b[par]], tag="xg_tr_ev")
        rsq_b = [Bf(f"rs_out{q}") for q in range(4)]
        for cc_ in range(4):
            em.do("gpsimd", lambda e, cc_=cc_: e.dma_start(
                out=wosl[cc_][:], in_=d_woT.ap()[cc_ * 128:(cc_ + 1) * 128, :]),
                wr=[wo_b[cc_]], dma=True, tag="wo_load")
        for q in range(4):
            for a in range(4):
                for j2 in range(2):
                    tb = 8 * a + 2 * q + j2
                    bk0 = 0
                    rg0 = (tb % 4) * 4
                    for cc_ in range(4):
                        lhs = yF[:].rearrange("p (c t) -> p c t", c=4)[:, cc_, tb * 128:(tb + 1) * 128]
                        for nb in range(4):
                            em.do("tensor", lambda e, lhs=lhs, cc_=cc_, nb=nb, bk0=bk0: e.matmul(
                                PS[bk0 + nb][:], lhs, wosl[cc_][:, nb * 512:(nb + 1) * 512],
                                start=(cc_ == 0), stop=(cc_ == 3)),
                                rd=[yF_b, wo_b[cc_]], wr=[pbuf[bk0 + nb]], tag="wo_mm")
                    r0 = 256 * a + 128 * j2
                    for nb in range(4):
                        ridx = rg0 + nb
                        bki = bk0 + nb
                        pv = pout8[:, ridx * 512:(ridx + 1) * 512]
                        em.do("vector", lambda e, bki=bki, pv=pv: e.tensor_scalar_mul(
                            pv, PS[bki][:], 1.0 - LAM0),
                            rd=[pbuf[bki]], wr=[po_b[ridx]], tag="wo_evac")
                        em.do(("sync", "gpsimd")[nb % 2],
                              lambda e, q=q, r0=r0, nb=nb, pv=pv: e.dma_start(
                                  out=d_pfq[q].ap()[r0:r0 + 128, nb * 512:(nb + 1) * 512], in_=pv),
                              rd=[po_b[ridx]], wr=[pfq_b[q]], dma=True, tag="p_store")
            if q < 2:
                em.do("gpsimd", lambda e, q=q: e.collective_compute(
                    "ReduceScatter", ALU.add, replica_groups=[[0, 1, 2, 3], [4, 5, 6, 7]],
                    ins=[d_pfq[q].ap()], outs=[d_rsq[q].ap()]),
                    rd=[pfq_b[q]], wr=[rsq_b[q]], cc=True, tag=f"rs{q}")
            if q < 3:
                emit_head(0, q)

        em.phase = "p52"
        h_b = [Bf("hsb0"), Bf("hsb1")]
        y2_b = [Bf("y2sb0"), Bf("y2sb1")]
        gelu_b = Bf("gelu_sc")
        y2d_b = Bf("y2e")
        em.overlay(y2_b, [kF_b] + po_b, tag="ovl_y2")
        em.overlay(h_b, [yF_b], tag="ovl_h")
        em.overlay(fc2sl_b, wo_b, tag="ovl_fc2b")

        for e_ in range(NE):
            par = e_ % 2
            if e_ == 1:
                for q in (2, 3):
                    em.do("gpsimd", lambda e, q=q: e.collective_compute(
                        "ReduceScatter", ALU.add, replica_groups=[[0, 1, 2, 3], [4, 5, 6, 7]],
                        ins=[d_pfq[q].ap()], outs=[d_rsq[q].ap()]),
                        rd=[pfq_b[q]], wr=[rsq_b[q]], cc=True, tag=f"rs{q}")
            if e_ != 0:
                emit_head(e_, 0)
                emit_head(e_, 1)
                emit_head(e_, 2)
            for ib in range(16):
                sl = ib % 4
                if not (e_ == 0 and ib < 4):
                    em.do("gpsimd", lambda e, e_=e_, ib=ib, sl=sl: e.dma_start(
                        out=fc1sl[sl][:], in_=d_fc1.ap()[e_, ib].rearrange("p jc i -> p (jc i)")),
                        wr=[fc1sl_b[sl]], dma=True, tag="fc1_load")
                for jc in range(16):
                    lhs = fc1sl[sl][:, jc * 128:(jc + 1) * 128]
                    rhs = xge[par][:, jc * CAP:(jc + 1) * CAP]
                    em.do("tensor", lambda e, lhs=lhs, rhs=rhs, ps_i=4 + ib % 2, jc=jc: e.matmul(
                        PS[ps_i][:, 0:CAP], lhs, rhs, start=(jc == 0), stop=(jc == 15)),
                        rd=[fc1sl_b[sl], xge_b[par]], wr=[pbuf[4 + ib % 2]], tag="fc1_mm")
                bcol = fc1b_sb[:, e_ * 16 + ib: e_ * 16 + ib + 1]
                hv = hsb[par][:, ib * CAP:(ib + 1) * CAP]
                em.do("scalar", lambda e, ps_i=4 + ib % 2, bcol=bcol, hv=hv: e.activation(
                    hv, PS[ps_i][:, 0:CAP], AF.Gelu, bias=bcol),
                    rd=[pbuf[4 + ib % 2], fc1b_b], wr=[h_b[par]], tag="gelu")
            for nb in range(4):
                for ibh in range(2):
                    sl = ibh
                    if True:
                        em.do("gpsimd", lambda e, e_=e_, nb=nb, ibh=ibh, sl=sl: e.dma_start(
                            out=fc2sl[sl][:],
                            in_=d_fc2.ap()[e_, nb, ibh].rearrange("p ib c -> p (ib c)")),
                            wr=[fc2sl_b[sl]], dma=True, tag="fc2_load")
                    for ib8 in range(8):
                        ib = ibh * 8 + ib8
                        for tb in range(NSB):
                            lhs = hsb[par][:, ib * CAP + tb * 128: ib * CAP + (tb + 1) * 128]
                            em.do("tensor", lambda e, lhs=lhs, sl=sl, ib8=ib8, tb=tb,
                                  first=(ib == 0): e.matmul(
                                      PS[tb][:], lhs, fc2sl[sl][:, ib8 * 512:(ib8 + 1) * 512],
                                      start=first, stop=False),
                                rd=[h_b[par], fc2sl_b[sl]], wr=[pbuf[tb]], tag="fc2_mm")
                for tb in range(NSB):
                    em.do("tensor", lambda e, e_=e_, tb=tb, nb=nb: e.matmul(
                        PS[tb][:], ind_sb[0:8, e_ * 128:(e_ + 1) * 128],
                        fc2b_sb[0:8, nb * 512:(nb + 1) * 512], start=False, stop=True),
                        rd=[ind_b, fc2b_b], wr=[pbuf[tb]], tag="b2_mm")
                for tb in range(NSB):
                    yv = y2sb[par][:, tb * D + nb * 512: tb * D + (nb + 1) * 512]
                    em.do("scalar", lambda e, tb=tb, yv=yv: e.copy(yv, PS[tb][:]),
                          rd=[pbuf[tb], eps_b], wr=[y2_b[par]], tag="y2_ev")
            for tb in range(NSB):
                em.do("sync", lambda e, e_=e_, tb=tb, par=par: e.dma_start(
                    out=d_y2e.ap()[e_ * CAP + tb * 128: e_ * CAP + (tb + 1) * 128, :],
                    in_=y2sb[par][:, tb * D:(tb + 1) * D]),
                    rd=[y2_b[par]], wr=[y2d_b], dma=True, tag="y2_store")

        em.phase = "p6"
        xr_b = [Bf("xr0"), Bf("xr1")]
        rt_b = [Bf("rt0"), Bf("rt1")]
        mg_b = [Bf(f"mgt{i}") for i in range(4)]
        em.overlay(xr_b + rt_b, h_b, tag="ovl_p6")
        em.overlay(mg_b, [vT_b] + y2_b + po_b, tag="ovl_mg")
        out_b = Bf("out_done")
        for tb in range(8):
            sl = tb % 2
            q, row0 = tb // 2, 128 * (tb % 2)
            for k_ in range(2):
                mi = k_ * 2 + sl
                em.do("gpsimd", lambda e, tb=tb, k_=k_, mi=mi: e.indirect_dma_start(
                    out=mgt[mi][:], out_offset=None, in_=d_y2e.ap(),
                    in_offset=bass.IndirectOffsetOnAxis(
                        ap=gidxi[:, tb * 2 + k_:tb * 2 + k_ + 1], axis=0),
                    oob_is_err=False),
                    rd=[gxi_b, y2d_b], wr=[mg_b[mi]], dma=True, tag="mg_gather")
            em.do("sync", lambda e, q=q, row0=row0, sl=sl: e.dma_start(
                out=rtile[sl][:], in_=d_rsq[q].ap()[row0:row0 + 128, :]),
                rd=[rsq_b[q]], wr=[rt_b[sl]], dma=True, tag="rs_load")
            em.do("sync", lambda e, tb=tb, sl=sl: e.dma_start(
                out=xrtile[sl][:], in_=d_xres.ap()[tb * 128:(tb + 1) * 128, :]),
                wr=[xr_b[sl]], dma=True, tag="xres_load")
            for k_ in range(2):
                mi = k_ * 2 + sl
                em.do("vector", lambda e, tb=tb, k_=k_, mi=mi, sl=sl: e.scalar_tensor_tensor(
                    xrtile[sl][:], mgt[mi][:], mergec[:, tb * 2 + k_:tb * 2 + k_ + 1],
                    xrtile[sl][:], ALU.mult, ALU.add),
                    rd=[mg_b[mi], mer_b, xr_b[sl]], wr=[xr_b[sl]], tag="merge")
            em.do("vector", lambda e, sl=sl: e.tensor_add(xrtile[sl][:], xrtile[sl][:], rtile[sl][:]),
                  rd=[rt_b[sl], xr_b[sl]], wr=[xr_b[sl]], tag="rs_add")
            em.do("sync", lambda e, tb=tb, sl=sl: e.dma_start(
                out=d_out.ap()[tb * 128:(tb + 1) * 128, :], in_=xrtile[sl][:]),
                rd=[xr_b[sl]], wr=[out_b], dma=True, tag="out_store")

        em.fence("scalar", [out_b], tag="final")
        em.final_fence("scalar")
        sem_of = {}
        for i, k in enumerate(sorted(em.all_sem_keys(), key=str)):
            sem_of[k] = en(nc.semaphore(f"S{i}"))
        em.finalize(block, sem_of)

    return nc, list(dbg_tensors.keys())



def _pm(a):
    return np.ascontiguousarray(a.reshape(16, 128, -1).transpose(1, 0, 2)).astype(np.float32)


def stage_inputs(inputs):
    x = np.asarray(inputs["L0_attn_x"], dtype=np.float32)
    wq = np.asarray(inputs["L0_attn_wq"], dtype=np.float32)
    wk = np.asarray(inputs["L0_attn_wk"], dtype=np.float32)
    wv = np.asarray(inputs["L0_attn_wv"], dtype=np.float32)
    wo = np.asarray(inputs["L0_attn_wo"], dtype=np.float32)
    lq1 = np.asarray(inputs["L0_attn_lq1"], dtype=np.float32)
    lk1 = np.asarray(inputs["L0_attn_lk1"], dtype=np.float32)
    lq2 = np.asarray(inputs["L0_attn_lq2"], dtype=np.float32)
    lk2 = np.asarray(inputs["L0_attn_lk2"], dtype=np.float32)
    gate_w = np.asarray(inputs["L0_ffn_gate_w"], dtype=np.float32)
    fc1_w = np.asarray(inputs["L0_ffn_fc1_w"], dtype=np.float32)
    fc1_b = np.asarray(inputs["L0_ffn_fc1_b"], dtype=np.float32)
    fc2_w = np.asarray(inputs["L0_ffn_fc2_w"], dtype=np.float32)
    fc2_b = np.asarray(inputs["L0_ffn_fc2_b"], dtype=np.float32)
    alpha = np.asarray(inputs["L0_ffn_alpha"], dtype=np.float32)

    fc1_s = np.ascontiguousarray(
        fc1_w.reshape(NE, 16, 128, 16, 128).transpose(0, 3, 2, 1, 4)).astype(np.float32)
    fc2_s = np.ascontiguousarray(
        fc2_w.reshape(NE, 2, 8, 128, 4, 512).transpose(0, 4, 1, 3, 2, 5))
    fc1b_s = np.ascontiguousarray(fc1_b.reshape(NE, 16, 128).transpose(2, 0, 1).reshape(128, NE * 16))
    fc2b_s = np.ascontiguousarray(fc2_b)
    gate_s = np.ascontiguousarray(gate_w.reshape(16, 128, NE).transpose(1, 0, 2))
    alpha_s = np.ascontiguousarray(np.broadcast_to(alpha[None, :], (128, NE)))
    lqk = np.ascontiguousarray(np.concatenate([lq1, lk1, lq2, lk2])[None, :].astype(np.float32))

    inv_freq = (1.0 / 10000.0) ** (np.arange(0, 128, 2, dtype=np.float32) / 128.0)

    in_maps = []
    for c in range(NCORES):
        b, r = c // 4, c % 4
        hpair = (2 * r, 2 * r + 1)
        xT = _pm(x[b].T)
        xresT = np.ascontiguousarray(xT[:, :, r * TSL:(r + 1) * TSL])
        xres = np.ascontiguousarray(x[b, r * TSL:(r + 1) * TSL, :])
        cs = slice(512 * r, 512 * r + 512)
        wqkvT = _pm(np.concatenate([wq.T[:, cs], wk.T[:, cs], wv.T[:, cs]], axis=1))
        woT = np.ascontiguousarray(wo.T[cs, :])
        cosr = np.zeros((128, 4, 64), dtype=np.float32)
        sinr = np.zeros((128, 4, 64), dtype=np.float32)
        for bi, hh in enumerate((hpair[0], hpair[0], hpair[1], hpair[1])):
            ang = hh * inv_freq
            cosr[:, bi, :] = np.cos(ang)[None, :]
            sinr[:, bi, :] = np.sin(ang)[None, :]
        ropes = np.ascontiguousarray(np.stack([cosr, sinr], axis=1))
        in_maps.append({
            "xT": xT, "xresT": xresT, "xres": xres, "wqkvT": wqkvT, "woT": woT,
            "ropes": ropes, "lqk": lqk, "gate_s": gate_s, "alpha_s": alpha_s,
            "fc1_s": fc1_s, "fc1b_s": fc1b_s, "fc2_s": fc2_s, "fc2b_s": fc2b_s,
        })
    return in_maps


def kernel(**inputs):
    nc, dbg_names = build_program(debug=tuple(DEBUG_DUMPS))
    in_maps = stage_inputs(inputs)
    res = run_bass_kernel_spmd(nc, in_maps, core_ids=list(range(NCORES)))
    outs = [np.asarray(res.results[c]["out"]) for c in range(NCORES)]
    full = np.stack(outs, axis=0).reshape(B, T, D).astype(np.float32)
    if dbg_names:
        kernel._dbg = [{n: np.asarray(res.results[c]["dbg_" + n]) for n in dbg_names}
                       for c in range(NCORES)]
    return full
